# version 08
# baseline (speedup 1.0000x reference)
import numpy as np
import ml_dtypes
from contextlib import ExitStack

import concourse.bass as bass
import concourse.mybir as mybir
from concourse.bass_utils import run_bass_kernel_spmd

F32 = mybir.dt.float32
BF16 = mybir.dt.bfloat16
AF = mybir.ActivationFunctionType
ALU = mybir.AluOpType
AX = mybir.AxisListType

NCORES = 8
B = 2
S = 4096
D = 4096
HD = 128
HH = 64
HF = 11008
TC = 1024
NDT = D // 128
NTT = TC // 128
NHT = HF // 128
SPC = 11
NSLOT = NCORES * SPC
HPC = 4

CORE_TILES = [11, 11, 11, 11, 11, 11, 10, 10]
CORE_TILE_START = [0, 11, 22, 33, 44, 55, 66, 76]
REAL_SLOTS = []
for _r in range(NCORES):
    for _k in range(CORE_TILES[_r]):
        REAL_SLOTS.append(11 * _r + _k)
assert len(REAL_SLOTS) == NHT

NF4_CODE = np.array([-1.0, -0.6961928009986877, -0.5250730514526367,
                     -0.39491748809814453, -0.28444138169288635, -0.18477343022823334,
                     -0.09105003625154495, 0.0, 0.07958029955625534, 0.16093020141124725,
                     0.24611230194568634, 0.33791524171829224, 0.44070982933044434,
                     0.5626170039176941, 0.7229568362236023, 1.0], dtype=np.float32)
NF4_BOUNDS = ((NF4_CODE[:-1] + NF4_CODE[1:]) * np.float32(0.5)).astype(np.float32)
NF4_GAMMA = ((NF4_CODE[1:] - NF4_CODE[:-1]) * np.float32(0.5)).astype(np.float32)
BLK = 64
NSC = 11

C_EPS, C_MHALF, C_INVD, C_SCALE, C_ONE = 0, 1, 2, 3, 4
C_NB0 = 5
C_G0 = 20
C_NEGM = 35
NCONST = 36


def _rope_tables():
    try:
        import jax
        import jax.numpy as jnp
        with jax.default_device(jax.devices("cpu")[0]):
            inv_freq = 1.0 / 10000.0 ** (jnp.arange(0, HD, 2, dtype=jnp.float32) / HD)
            pos = jnp.arange(S, dtype=jnp.float32)
            freqs = pos[:, None] * inv_freq[None, :]
            emb = jnp.concatenate([freqs, freqs], axis=-1)
            cos_np = np.asarray(jax.device_get(jnp.cos(emb)), dtype=np.float32)
            sin_np = np.asarray(jax.device_get(jnp.sin(emb)), dtype=np.float32)
        return cos_np, sin_np
    except Exception:
        inv_freq = (np.float32(1.0) /
                    np.float32(10000.0) ** (np.arange(0, HD, 2, dtype=np.float32) / np.float32(HD)))
        pos = np.arange(S, dtype=np.float32)
        freqs = (pos[:, None] * inv_freq[None, :]).astype(np.float32)
        emb = np.concatenate([freqs, freqs], axis=-1).astype(np.float32)
        return np.cos(emb).astype(np.float32), np.sin(emb).astype(np.float32)


def _const_bank():
    cb = np.zeros((128, NCONST), dtype=np.float32)
    cb[:, C_EPS] = np.float32(1e-5)
    cb[:, C_MHALF] = np.float32(-0.5)
    cb[:, C_INVD] = np.float32(1.0 / 4096.0)
    cb[:, C_SCALE] = np.float32(1.0) / np.float32(np.sqrt(np.float32(HD)))
    cb[:, C_ONE] = np.float32(1.0)
    for j in range(15):
        cb[:, C_NB0 + j] = -NF4_BOUNDS[j]
        cb[:, C_G0 + j] = NF4_GAMMA[j]
    cb[:, C_NEGM] = np.float32(-10.0)
    return cb


def _mask_tile():
    m = np.zeros((128, 128), dtype=np.float32)
    m[np.triu_indices(128, 1)] = np.float32(-1.0e30)
    return m


class Ctr:
    def __init__(self, stack, nc, name, step=1):
        self.h = stack.enter_context(nc.semaphore(name))
        self.n = 0
        self.step = step

    def tick(self, instr):
        self.n += self.step
        instr.then_inc(self.h, self.step)
        return self.n

    def wait(self, eng, val=None):
        eng.wait_ge(self.h, self.n if val is None else val)


def build_program(nphases=99, taps=(), detect_races=True, skip_ag=False):
    P1LVL = 99
    nc = bass.Bass("TRN2", target_bir_lowering=False, detect_race_conditions=detect_races)
    taps = list(taps)

    x_own = nc.dram_tensor("x_own", [TC, D], F32, kind="ExternalInput").ap()
    wq_sl = nc.dram_tensor("wq_sl", [D, 512], F32, kind="ExternalInput").ap()
    wk_sl = nc.dram_tensor("wk_sl", [D, 512], F32, kind="ExternalInput").ap()
    wv_sl = nc.dram_tensor("wv_sl", [D, 512], F32, kind="ExternalInput").ap()
    wo_full = nc.dram_tensor("wo_full", [D, D], F32, kind="ExternalInput").ap()
    w1_sh = nc.dram_tensor("w1_sh", [SPC * 128, D], F32, kind="ExternalInput").ap()
    w2_sh = nc.dram_tensor("w2_sh", [SPC * 128, D], F32, kind="ExternalInput").ap()
    w3_sh = nc.dram_tensor("w3_sh", [D, SPC * 128], F32, kind="ExternalInput").ap()
    cos_t = nc.dram_tensor("cos_t", [S, HD], F32, kind="ExternalInput").ap()
    sin_t = nc.dram_tensor("sin_t", [S, HD], F32, kind="ExternalInput").ap()
    cbank = nc.dram_tensor("cbank", [128, NCONST], F32, kind="ExternalInput").ap()
    ident_in = nc.dram_tensor("ident_in", [128, 128], F32, kind="ExternalInput").ap()
    mask_in = nc.dram_tensor("mask_in", [128, 128], F32, kind="ExternalInput").ap()
    cmask_in = nc.dram_tensor("cmask_in", [128, 8], F32, kind="ExternalInput").ap()

    out = nc.dram_tensor("out", [TC, D], F32, kind="ExternalOutput").ap()

    xnt_c = nc.dram_tensor("xnt_c", [D, TC], BF16, kind="Internal").ap()
    xnt_all = nc.dram_tensor("xnt_all", [NCORES * D, TC], BF16, kind="Internal").ap()
    dq1t_c = nc.dram_tensor("dq1t_c", [SPC * 128, D], BF16, kind="Internal").ap()
    dq2t_c = nc.dram_tensor("dq2t_c", [SPC * 128, D], BF16, kind="Internal").ap()
    dq3t_c = nc.dram_tensor("dq3t_c", [SPC * 128, D], BF16, kind="Internal").ap()
    dq1t_all = nc.dram_tensor("dq1t_all", [NSLOT * 128, D], BF16, kind="Internal").ap()
    dq2t_all = nc.dram_tensor("dq2t_all", [NSLOT * 128, D], BF16, kind="Internal").ap()
    dq3t_all = nc.dram_tensor("dq3t_all", [NSLOT * 128, D], BF16, kind="Internal").ap()
    qt_sp = nc.dram_tensor("qt_sp", [B * HPC * HD, S], BF16, kind="Internal").ap()
    kt_sp = nc.dram_tensor("kt_sp", [B * HPC * HD, S], BF16, kind="Internal").ap()
    v_sp = nc.dram_tensor("v_sp", [B * HPC * 128, S], BF16, kind="Internal").ap()
    ctx_c = nc.dram_tensor("ctx_c", [B * 512, S], BF16, kind="Internal").ap()
    ctx_all = nc.dram_tensor("ctx_all", [NCORES * B * 512, S], BF16, kind="Internal").ap()
    y_hbm = nc.dram_tensor("y_hbm", [TC, D], F32, kind="Internal").ap()

    TAP_SRC = {
        "xnt_all": (xnt_all, [NCORES * D, TC], BF16),
        "dq1t_all": (dq1t_all, [NSLOT * 128, D], BF16),
        "dq2t_all": (dq2t_all, [NSLOT * 128, D], BF16),
        "dq3t_all": (dq3t_all, [NSLOT * 128, D], BF16),
        "qt_sp": (qt_sp, [B * HPC * HD, S], BF16),
        "kt_sp": (kt_sp, [B * HPC * HD, S], BF16),
        "v_sp": (v_sp, [B * HPC * 128, S], BF16),
        "ctx_all": (ctx_all, [NCORES * B * 512, S], BF16),
        "y_hbm": (y_hbm, [TC, D], F32),
    }
    tap_out = {}
    for name in taps:
        src, shape, dt = TAP_SRC[name]
        tap_out[name] = (nc.dram_tensor("tap_" + name, shape, dt, kind="ExternalOutput").ap(), src, shape)

    st = ExitStack()
    with st:
        consts = st.enter_context(nc.sbuf_tensor("consts", [128, NCONST], F32))
        ident_f = st.enter_context(nc.sbuf_tensor("ident_f", [128, 128], F32))
        ident = st.enter_context(nc.sbuf_tensor("ident", [128, 128], BF16))
        maskt = st.enter_context(nc.sbuf_tensor("maskt", [128, 128], F32))

        ld = Ctr(st, nc, "ld", 16)
        stw = Ctr(st, nc, "stw", 16)
        gld = Ctr(st, nc, "gld", 16)
        ldx = [Ctr(st, nc, "ldx0", 16), Ctr(st, nc, "ldx1", 16)]
        stxnt = Ctr(st, nc, "stxnt", 16)
        ldw = Ctr(st, nc, "ldw", 16)
        stdq = Ctr(st, nc, "stdq", 16)
        gsem = Ctr(st, nc, "gsem")
        gwc = Ctr(st, nc, "gwc", 16)
        ldxc = Ctr(st, nc, "ldxc", 16)
        stqk = Ctr(st, nc, "stqk", 16)
        ldh = [Ctr(st, nc, "ldh0", 16), Ctr(st, nc, "ldh1", 16)]
        stcx = Ctr(st, nc, "stcx", 16)
        ldxn4 = Ctr(st, nc, "ldxn4", 16)
        ldd12 = [Ctr(st, nc, "ldd12a", 16), Ctr(st, nc, "ldd12b", 16)]
        ldd3 = [Ctr(st, nc, "ldd3a", 16), Ctr(st, nc, "ldd3b", 16)]
        sty = Ctr(st, nc, "sty", 16)
        gy = Ctr(st, nc, "gy", 16)
        ldcd = [Ctr(st, nc, "ldcd0", 16), Ctr(st, nc, "ldcd1", 16)]
        gwo = [Ctr(st, nc, "gwo0", 16), Ctr(st, nc, "gwo1", 16)]
        ldyx = [Ctr(st, nc, "ldyx0", 16), Ctr(st, nc, "ldyx1", 16)]
        vsem = Ctr(st, nc, "vsem")
        ssem = Ctr(st, nc, "ssem")
        pesem = Ctr(st, nc, "pesem")
        cc_xnt = Ctr(st, nc, "cc_xnt")
        cc_dq = Ctr(st, nc, "cc_dq")
        cc_ctx = Ctr(st, nc, "cc_ctx")
        outsem = Ctr(st, nc, "outsem", 16)

        block = st.enter_context(nc.Block())

        @block.sync
        def _(sync):
            sync.dma_start(out=consts[:], in_=cbank[:]).then_inc(ld.h, 16)
            sync.dma_start(out=ident_f[:], in_=ident_in[:]).then_inc(ld.h, 16)
            sync.dma_start(out=maskt[:], in_=mask_in[:]).then_inc(ld.h, 16)
        ld.n += 48

        maskb = st.enter_context(nc.sbuf_tensor("maskb", [128, 128], BF16))

        @block.vector
        def _(v):
            v.wait_ge(ld.h, ld.n)
            v.tensor_copy(ident[:], ident_f[:])
            vsem.tick(v.tensor_copy(maskb[:], maskt[:]))
        ident_ready = vsem.n
        consts_ready = ld.n

        @block.tensor
        def _(t):
            t.wait_ge(vsem.h, ident_ready)

        with ExitStack() as ph:
            xs = [ph.enter_context(nc.sbuf_tensor(f"p0_x{i}", [128, D], F32)) for i in range(2)]
            xc = ph.enter_context(nc.sbuf_tensor("p0_xc", [128, D], F32))
            sq = ph.enter_context(nc.sbuf_tensor("p0_sq", [128, D], F32))
            stats = ph.enter_context(nc.sbuf_tensor("p0_st", [128, 8], F32))
            xnb = ph.enter_context(nc.sbuf_tensor("p0_xnb", [128, D], BF16))
            ev = ph.enter_context(nc.sbuf_tensor("p0_ev", [128, 1024], BF16))
            tps = ph.enter_context(nc.psum_tensor("p0_tps", [128, 1024], BF16))

            v_xc_done = []
            v_evac = {}
            pe_tp = {}
            s_rstd = []
            ev_st = {}

            for tt in range(NTT):
                @block.sync
                def _(sync, tt=tt):
                    if tt == 0:
                        sync.wait_ge(ld.h, consts_ready)
                    if tt >= 2:
                        sync.wait_ge(vsem.h, v_xc_done[tt - 2])
                    ldx[tt % 2].tick(sync.dma_start(out=xs[tt % 2][:], in_=x_own[tt * 128:(tt + 1) * 128, :]))
                x_ld = ldx[tt % 2].n

                @block.vector
                def _(v, tt=tt, x_ld=x_ld):
                    v.wait_ge(ldx[tt % 2].h, x_ld)
                    if tt > 0:
                        v.wait_ge(ssem.h, s_rstd[tt - 1])
                    vsem.tick(v.tensor_reduce(stats[:, 0:1], xs[tt % 2][:], AX.X, ALU.add))
                    v.wait_ge(vsem.h, vsem.n)
                    vsem.tick(v.tensor_scalar(stats[:, 1:2], stats[:, 0:1], consts[:, C_INVD:C_INVD + 1], None, ALU.mult))
                    v.wait_ge(vsem.h, vsem.n)
                    vsem.tick(v.tensor_scalar(xc[:], xs[tt % 2][:], stats[:, 1:2], None, ALU.subtract))
                v_xc_done.append(vsem.n)

                @block.scalar
                def _(s, tt=tt):
                    s.wait_ge(vsem.h, v_xc_done[tt])
                    ssem.tick(s.activation(sq[:], xc[:], AF.Square, accum_out=stats[:, 2:3]))
                    s.wait_ge(ssem.h, ssem.n)
                    ssem.tick(s.mul(stats[:, 3:4], stats[:, 2:3], consts[:, C_INVD:C_INVD + 1]))
                    s.wait_ge(ssem.h, ssem.n)
                    ssem.tick(s.activation(stats[:, 4:5], stats[:, 3:4], AF.Ln, bias=consts[:, C_EPS:C_EPS + 1]))
                    s.wait_ge(ssem.h, ssem.n)
                    ssem.tick(s.activation(stats[:, 5:6], stats[:, 4:5], AF.Exp,
                                           scale=consts[:, C_MHALF:C_MHALF + 1]))
                s_rstd.append(ssem.n)

                @block.vector
                def _(v, tt=tt):
                    v.wait_ge(ssem.h, s_rstd[tt])
                    if tt > 0:
                        v.wait_ge(pesem.h, pe_tp[(tt - 1, 3)])
                    vsem.tick(v.tensor_scalar(xnb[:], xc[:], stats[:, 5:6], None, ALU.mult))
                v_xnb = vsem.n

                for g in range(4):
                    @block.tensor
                    def _(t, tt=tt, g=g, v_xnb=v_xnb):
                        t.wait_ge(vsem.h, v_xnb)
                        if (tt, g) != (0, 0):
                            prev = (tt, g - 1) if g > 0 else (tt - 1, 3)
                            t.wait_ge(ssem.h, v_evac[prev])
                        last = None
                        for i in range(8):
                            dt = g * 8 + i
                            last = t.transpose(tps[:, i * 128:(i + 1) * 128],
                                               xnb[:, dt * 128:(dt + 1) * 128], ident[:])
                        pesem.tick(last)
                    pe_tp[(tt, g)] = pesem.n

                    @block.scalar
                    def _(sc, tt=tt, g=g):
                        sc.wait_ge(pesem.h, pe_tp[(tt, g)])
                        if (tt, g) != (0, 0):
                            prev = (tt, g - 1) if g > 0 else (tt - 1, 3)
                            sc.wait_ge(stxnt.h, ev_st[prev])
                        ssem.tick(sc.copy(ev[:], tps[:]))
                    v_evac[(tt, g)] = ssem.n

                    @block.sync
                    def _(sync, tt=tt, g=g):
                        sync.wait_ge(ssem.h, v_evac[(tt, g)])
                        for i in range(8):
                            dt = g * 8 + i
                            stxnt.tick(sync.dma_start(
                                out=xnt_c[dt * 128:(dt + 1) * 128, tt * 128:(tt + 1) * 128],
                                in_=ev[:, i * 128:(i + 1) * 128]))
                    ev_st[(tt, g)] = stxnt.n

            xnt_stores = stxnt.n

            if not skip_ag:
                @block.gpsimd
                def _(g):
                    g.wait_ge(stxnt.h, xnt_stores)
                    cc_xnt.tick(g.collective_compute(
                        "AllGather", ALU.bypass,
                        replica_groups=[list(range(NCORES))],
                        ins=[xnt_c[:].opt()], outs=[xnt_all[:].opt()]))

        if nphases >= 1:
            with ExitStack() as ph:
                wrow = [ph.enter_context(nc.sbuf_tensor(f"dq_w{i}", [128, D], F32)) for i in range(2)]
                nrm = ph.enter_context(nc.sbuf_tensor("dq_n", [128, D], F32))
                term = [ph.enter_context(nc.sbuf_tensor(f"dq_t{i}", [128, D], F32)) for i in range(3)]
                vacc = ph.enter_context(nc.sbuf_tensor("dq_va", [128, D], F32))
                acc = ph.enter_context(nc.sbuf_tensor("dq_a", [128, D], F32))
                dqb = ph.enter_context(nc.sbuf_tensor("dq_b", [128, D], BF16))
                am = ph.enter_context(nc.sbuf_tensor("dq_am", [128, D // BLK], F32))
                am2 = ph.enter_context(nc.sbuf_tensor("dq_am2", [128, D // BLK], F32))
                ev = ph.enter_context(nc.sbuf_tensor("dq_ev", [128, 1024], BF16))
                tps = ph.enter_context(nc.psum_tensor("dq_tps", [128, 1024], BF16))

                state = {"v_nrm": 0, "pe_last": 0, "s_last_sign": 0,
                         "dq_ev_st": 0, "v_ev_prev": 0}

                def dq_tile(src_ap, ncols, writes, tile_id):
                    nblk = ncols // BLK
                    ngrp = ncols // 128
                    r3 = lambda ap: ap.rearrange("p (n b) -> p n b", b=BLK)

                    wr = wrow[tile_id % 2]

                    @block.sync
                    def _(sync):
                        if tile_id > 1:
                            sync.wait_ge(vsem.h, state.get(f"v_nrm_{tile_id % 2}", 0))
                        ldw.tick(sync.dma_start(out=wr[:, 0:ncols], in_=src_ap))
                    w_ld = ldw.n

                    @block.vector
                    def _(v):
                        v.wait_ge(ldw.h, w_ld)
                        if tile_id > 0:
                            v.wait_ge(pesem.h, state["pe_last"])
                            v.wait_ge(ssem.h, state["s_last_sign"])
                        vsem.tick(v.tensor_reduce(am[:, 0:nblk], r3(wr[:, 0:ncols]), AX.X, ALU.max,
                                                   apply_absolute_value=True))
                        v.wait_ge(vsem.h, vsem.n)
                        vsem.tick(v.tensor_scalar(am2[:, 0:nblk], am[:, 0:nblk], 0.0, None, ALU.is_equal))
                        v.wait_ge(vsem.h, vsem.n)
                        vsem.tick(v.tensor_tensor(am2[:, 0:nblk], am[:, 0:nblk], am2[:, 0:nblk], ALU.add))
                        v.wait_ge(vsem.h, vsem.n)
                        vsem.tick(v.reciprocal(am2[:, 0:nblk], am2[:, 0:nblk]))
                        v.wait_ge(vsem.h, vsem.n)
                        vsem.tick(v.tensor_tensor(
                            r3(nrm[:, 0:ncols]), r3(wr[:, 0:ncols]),
                            am2[:, 0:nblk].rearrange("p (n o) -> p n o", o=1).broadcast_to([128, nblk, BLK]),
                            ALU.mult))
                    state["v_nrm"] = vsem.n
                    state[f"v_nrm_{tile_id % 2}"] = vsem.n
                    v_nrm = vsem.n

                    NGP = 0
                    v_term = {}
                    g_add = {}
                    for j in range(15):
                        tb = term[j % 3]

                        @block.vector
                        def _(v, j=j, tb=tb):
                            if j == 0 and gsem.n:
                                v.wait_ge(gsem.h, gsem.n)
                            if j >= 3 and (j - 3) < NGP:
                                v.wait_ge(gsem.h, g_add[j - 3])
                            vsem.tick(v.tensor_scalar(tb[:, 0:ncols], nrm[:, 0:ncols],
                                                      float(NF4_BOUNDS[j]), float(2.0 * NF4_GAMMA[j]),
                                                      ALU.is_gt, ALU.mult))
                        v_term[j] = vsem.n

                        if j < NGP:
                            @block.gpsimd
                            def _(g, j=j, tb=tb):
                                g.wait_ge(vsem.h, v_term[j])
                                if j == 0:
                                    gsem.tick(g.tensor_copy(acc[:, 0:ncols], tb[:, 0:ncols]))
                                else:
                                    gsem.tick(g.tensor_tensor(acc[:, 0:ncols], acc[:, 0:ncols],
                                                              tb[:, 0:ncols], ALU.add))
                            g_add[j] = gsem.n
                        else:
                            @block.vector
                            def _(v, j=j, tb=tb):
                                if j == NGP:
                                    cv = -float(np.sum(NF4_GAMMA, dtype=np.float64))
                                    vsem.tick(v.tensor_scalar(vacc[:, 0:ncols], tb[:, 0:ncols],
                                                              cv, None, ALU.add))
                                else:
                                    vsem.tick(v.tensor_tensor(vacc[:, 0:ncols], vacc[:, 0:ncols],
                                                              tb[:, 0:ncols], ALU.add))
                            v_term[j] = vsem.n
                    state["s_last_sign"] = ssem.n

                    @block.vector
                    def _(v):
                        if NGP > 0:
                            v.wait_ge(gsem.h, g_add[NGP - 1])
                            v.tensor_tensor(acc[:, 0:ncols], acc[:, 0:ncols], vacc[:, 0:ncols], ALU.add)
                            v.wait_ge(vsem.h, vsem.n)
                            src_acc = acc
                        else:
                            src_acc = vacc
                        vsem.tick(v.tensor_tensor(
                            r3(dqb[:, 0:ncols]), r3(src_acc[:, 0:ncols]),
                            am[:, 0:nblk].rearrange("p (n o) -> p n o", o=1).broadcast_to([128, nblk, BLK]),
                            ALU.mult))
                    v_dqb = vsem.n

                    ngroups = (ngrp + 7) // 8
                    for g in range(ngroups):
                        n_in_g = min(8, ngrp - g * 8)

                        @block.tensor
                        def _(t, g=g, n_in_g=n_in_g):
                            t.wait_ge(vsem.h, v_dqb)
                            if state["v_ev_prev"]:
                                t.wait_ge(ssem.h, state["v_ev_prev"])
                            last = None
                            for i in range(n_in_g):
                                c = (g * 8 + i) * 128
                                last = t.transpose(tps[:, i * 128:(i + 1) * 128],
                                                   dqb[:, c:c + 128], ident[:])
                            pesem.tick(last)
                        pe_g = pesem.n

                        @block.scalar
                        def _(sc, n_in_g=n_in_g, pe_g=pe_g):
                            sc.wait_ge(pesem.h, pe_g)
                            if state["dq_ev_st"]:
                                sc.wait_ge(stdq.h, state["dq_ev_st"])
                            ssem.tick(sc.copy(ev[:, 0:n_in_g * 128], tps[:, 0:n_in_g * 128]))
                        state["v_ev_prev"] = ssem.n
                        v_ev = ssem.n

                        @block.sync
                        def _(sync, g=g, n_in_g=n_in_g, v_ev=v_ev):
                            sync.wait_ge(ssem.h, v_ev)
                            for i in range(n_in_g):
                                stdq.tick(sync.dma_start(out=writes[g * 8 + i],
                                                         in_=ev[:, i * 128:(i + 1) * 128]))
                        state["dq_ev_st"] = stdq.n
                    state["pe_last"] = pesem.n

                tile_id = 0
                for wsrc, dqdst in ((w1_sh, dq1t_c), (w2_sh, dq2t_c)):
                    for k in range(SPC):
                        slot_rows = slice(k * 128, (k + 1) * 128)
                        writes = [dqdst[k * 128:(k + 1) * 128, dt * 128:(dt + 1) * 128] for dt in range(NDT)]
                        dq_tile(wsrc[k * 128:(k + 1) * 128, :], D, writes, tile_id)
                        tile_id += 1
                for dti in range(NDT):
                    writes = [dq3t_c[k * 128:(k + 1) * 128, dti * 128:(dti + 1) * 128] for k in range(SPC)]
                    dq_tile(w3_sh[dti * 128:(dti + 1) * 128, :], SPC * 128, writes, tile_id)
                    tile_id += 1

                dq_stores = stdq.n

                @block.gpsimd
                def _(g):
                    g.wait_ge(stdq.h, dq_stores)
                    cc_dq.tick(g.collective_compute(
                        "AllGather", ALU.bypass, replica_groups=[list(range(NCORES))],
                        ins=[dq1t_c[:].opt()], outs=[dq1t_all[:].opt()]))
                    cc_dq.tick(g.collective_compute(
                        "AllGather", ALU.bypass, replica_groups=[list(range(NCORES))],
                        ins=[dq2t_c[:].opt()], outs=[dq2t_all[:].opt()]))
                    cc_dq.tick(g.collective_compute(
                        "AllGather", ALU.bypass, replica_groups=[list(range(NCORES))],
                        ins=[dq3t_c[:].opt()], outs=[dq3t_all[:].opt()]))

        if nphases >= 2:
            for b in range(B):
                with ExitStack() as ph:
                    wqb = ph.enter_context(nc.sbuf_tensor(f"p1_wq_{b}", [128, NDT * 512], BF16))
                    wkb = ph.enter_context(nc.sbuf_tensor(f"p1_wk_{b}", [128, NDT * 512], BF16))
                    wvb = ph.enter_context(nc.sbuf_tensor(f"p1_wv_{b}", [128, NDT * 512], BF16))
                    xcb = ph.enter_context(nc.sbuf_tensor(f"p1_xcb_{b}", [128, NDT * 1024], BF16))
                    cosb = ph.enter_context(nc.sbuf_tensor(f"p1_cos_{b}", [128, 8 * 128], F32))
                    sinb = ph.enter_context(nc.sbuf_tensor(f"p1_sin_{b}", [128, 8 * 128], F32))
                    qf = ph.enter_context(nc.sbuf_tensor(f"p1_qf_{b}", [128, 512], F32))
                    kf = ph.enter_context(nc.sbuf_tensor(f"p1_kf_{b}", [128, 512], F32))
                    rtmpA = ph.enter_context(nc.sbuf_tensor(f"p1_tA_{b}", [128, 256], F32))
                    rtmpB = ph.enter_context(nc.sbuf_tensor(f"p1_tB_{b}", [128, 256], F32))
                    qbt = ph.enter_context(nc.sbuf_tensor(f"p1_qb_{b}", [128, 512], BF16))
                    kbt = ph.enter_context(nc.sbuf_tensor(f"p1_kb_{b}", [128, 512], BF16))
                    vbt = ph.enter_context(nc.sbuf_tensor(f"p1_vb_{b}", [128, 512], BF16))
                    ev = ph.enter_context(nc.sbuf_tensor(f"p1_ev_{b}", [128, 1024], BF16))
                    qps = ph.enter_context(nc.psum_tensor(f"p1_qps_{b}", [128, 512], F32))
                    kps = ph.enter_context(nc.psum_tensor(f"p1_kps_{b}", [128, 512], F32))
                    vps = ph.enter_context(nc.psum_tensor(f"p1_vps_{b}", [128, 512], F32))
                    tps = ph.enter_context(nc.psum_tensor(f"p1_tps_{b}", [128, 1024], BF16))

                    pe_now, v_now, s_now = pesem.n, vsem.n, ssem.n

                    @block.gpsimd
                    def _(g, pe_now=pe_now, v_now=v_now, s_now=s_now):
                        g.wait_ge(pesem.h, pe_now)
                        g.wait_ge(vsem.h, v_now)
                        g.wait_ge(ssem.h, s_now)
                        for wsb, wsrc in ((wqb, wq_sl), (wkb, wk_sl), (wvb, wv_sl)):
                            for dt in range(NDT):
                                gwc.tick(g.dma_start(out=wsb[:, dt * 512:(dt + 1) * 512],
                                                     in_=wsrc[dt * 128:(dt + 1) * 128, :]))
                    w_ld = gwc.n

                    st_evac = {}
                    st_tp = {}
                    st_store = {}
                    v_rope = {}
                    pe_mm = {}
                    pe_tp = {}

                    for cb in range(4):
                        blk = b * 4 + cb

                        @block.sync
                        def _(sync, cb=cb, blk=blk):
                            if (b, cb) != (0, 0):
                                sync.wait_ge(pesem.h, pesem.n)
                            for dt in range(NDT):
                                ldxc.tick(sync.dma_start(
                                    out=xcb[:, dt * 1024:(dt + 1) * 1024],
                                    in_=xnt_all[blk * D + dt * 128: blk * D + (dt + 1) * 128, :]))
                            cs = cos_t[cb * 1024:(cb + 1) * 1024, :].rearrange("(t p) h -> p t h", p=128)
                            sn = sin_t[cb * 1024:(cb + 1) * 1024, :].rearrange("(t p) h -> p t h", p=128)
                            ldxc.tick(sync.dma_start(out=cosb[:].rearrange("p (t h) -> p t h", h=128), in_=cs))
                            ldxc.tick(sync.dma_start(out=sinb[:].rearrange("p (t h) -> p t h", h=128), in_=sn))
                        x_ld = ldxc.n

                        for tt in range(NTT):
                            if P1LVL < 3:
                                continue
                            @block.tensor
                            def _(t, cb=cb, tt=tt, x_ld=x_ld):
                                t.wait_ge(ldxc.h, x_ld)
                                if (cb, tt) == (0, 0):
                                    t.wait_ge(gwc.h, w_ld)
                                prev = (cb, tt - 1) if tt > 0 else ((cb - 1, NTT - 1) if cb > 0 else None)
                                if prev is not None and st_evac.get(prev):
                                    t.wait_ge(ssem.h, st_evac[prev])
                                last = None
                                for dt in range(NDT):
                                    lt = xcb[:, dt * 1024 + tt * 128: dt * 1024 + (tt + 1) * 128]
                                    st_ = (dt == 0)
                                    sp = (dt == NDT - 1)
                                    t.matmul(qps[:], lt, wqb[:, dt * 512:(dt + 1) * 512], start=st_, stop=sp)
                                    t.matmul(kps[:], lt, wkb[:, dt * 512:(dt + 1) * 512], start=st_, stop=sp)
                                    last = t.matmul(vps[:], lt, wvb[:, dt * 512:(dt + 1) * 512], start=st_, stop=sp)
                                pesem.tick(last)
                            pe_mm[(cb, tt)] = pesem.n

                            if P1LVL >= 35:
                                @block.scalar
                                def _(s, cb=cb, tt=tt):
                                    s.wait_ge(pesem.h, pe_mm[(cb, tt)])
                                    prev = (cb, tt - 1) if tt > 0 else ((cb - 1, NTT - 1) if cb > 0 else None)
                                    if prev is not None and st_store.get(prev):
                                        s.wait_ge(stqk.h, st_store[prev])
                                    if (cb, tt) != (0, 0) and vsem.n:
                                        s.wait_ge(vsem.h, vsem.n)
                                    if P1LVL == 34:
                                        ssem.tick(s.copy(qf[:, 0:NCONST], consts[:]))
                                    else:
                                        ssem.tick(s.copy(qf[:], qps[:]))
                                    if P1LVL >= 36:
                                        ssem.tick(s.copy(kf[:], kps[:]))
                                        ssem.tick(s.copy(vbt[:], vps[:]))
                                st_evac[(cb, tt)] = ssem.n
                            else:
                                st_evac[(cb, tt)] = ssem.n

                            if P1LVL < 40:
                                continue
                            @block.vector
                            def _(v, cb=cb, tt=tt):
                                v.wait_ge(ssem.h, st_evac[(cb, tt)])
                                if (cb, tt) != (0, 0):
                                    v.wait_ge(pesem.h, pesem.n)
                                for src, dst in ((qf, qbt), (kf, kbt)):
                                    s4 = src[:].rearrange("p (h a x) -> p h a x", h=HPC, a=2)
                                    d4 = dst[:].rearrange("p (h a x) -> p h a x", h=HPC, a=2)
                                    c4 = cosb[:, tt * 128:(tt + 1) * 128].rearrange("p (o a x) -> p o a x", o=1, a=2)
                                    s4t = sinb[:, tt * 128:(tt + 1) * 128].rearrange("p (o a x) -> p o a x", o=1, a=2)
                                    c1 = c4[:, :, 0, :].broadcast_to([128, HPC, HH])
                                    c2 = c4[:, :, 1, :].broadcast_to([128, HPC, HH])
                                    s1 = s4t[:, :, 0, :].broadcast_to([128, HPC, HH])
                                    s2 = s4t[:, :, 1, :].broadcast_to([128, HPC, HH])
                                    tA = rtmpA[:].rearrange("p (h x) -> p h x", h=HPC)
                                    tB = rtmpB[:].rearrange("p (h x) -> p h x", h=HPC)
                                    q1 = s4[:, :, 0, :]
                                    q2 = s4[:, :, 1, :]
                                    v.tensor_tensor(tA, q1, c1, ALU.mult)
                                    v.tensor_tensor(tB, q2, s1, ALU.mult)
                                    v.tensor_tensor(d4[:, :, 0, :], tA, tB, ALU.subtract)
                                    v.tensor_tensor(tA, q2, c2, ALU.mult)
                                    v.tensor_tensor(tB, q1, s2, ALU.mult)
                                    last = v.tensor_tensor(d4[:, :, 1, :], tA, tB, ALU.add)
                                vsem.tick(last)
                            v_rope[(cb, tt)] = vsem.n

                            if P1LVL < 50:
                                continue
                            @block.tensor
                            def _(t, cb=cb, tt=tt):
                                t.wait_ge(vsem.h, v_rope[(cb, tt)])
                                if (cb, tt) != (0, 0):
                                    t.wait_ge(ssem.h, ssem.n)
                                for h in range(HPC):
                                    t.transpose(tps[:, h * 128:(h + 1) * 128],
                                                qbt[:, h * 128:(h + 1) * 128], ident[:])
                                last = None
                                for h in range(HPC):
                                    last = t.transpose(tps[:, 512 + h * 128: 512 + (h + 1) * 128],
                                                       kbt[:, h * 128:(h + 1) * 128], ident[:])
                                pesem.tick(last)
                            pe_tp[(cb, tt)] = pesem.n

                            @block.scalar
                            def _(s, cb=cb, tt=tt):
                                s.wait_ge(pesem.h, pe_tp[(cb, tt)])
                                if (cb, tt) != (0, 0):
                                    s.wait_ge(stqk.h, stqk.n)
                                ssem.tick(s.copy(ev[:], tps[:]))
                            st_tp[(cb, tt)] = ssem.n

                            @block.sync
                            def _(sync, cb=cb, tt=tt, blk=blk):
                                sync.wait_ge(ssem.h, st_tp[(cb, tt)])
                                col = cb * 1024 + tt * 128
                                for h in range(HPC):
                                    r0 = (b * HPC + h) * HD
                                    stqk.tick(sync.dma_start(
                                        out=qt_sp[r0:r0 + 128, col:col + 128],
                                        in_=ev[:, h * 128:(h + 1) * 128]))
                                    stqk.tick(sync.dma_start(
                                        out=kt_sp[r0:r0 + 128, col:col + 128],
                                        in_=ev[:, 512 + h * 128: 512 + (h + 1) * 128]))
                                    kt_glob = cb * 8 + tt
                                    vr = (b * HPC + h) * 128
                                    stqk.tick(sync.dma_start(
                                        out=v_sp[vr:vr + 128, kt_glob * 128:(kt_glob + 1) * 128],
                                        in_=vbt[:, h * 128:(h + 1) * 128]))
                            st_store[(cb, tt)] = stqk.n

                    p1_stores = stqk.n
                    p1_pe = pesem.n

                if nphases < 3:
                    continue
                with ExitStack() as ph:
                    qth = [ph.enter_context(nc.sbuf_tensor(f"p2_q{b}_{i}", [128, S], BF16)) for i in range(2)]
                    kth = [ph.enter_context(nc.sbuf_tensor(f"p2_k{b}_{i}", [128, S], BF16)) for i in range(2)]
                    vh = [ph.enter_context(nc.sbuf_tensor(f"p2_v{b}_{i}", [128, 32 * 128], BF16)) for i in range(2)]
                    perow = [ph.enter_context(nc.sbuf_tensor(f"p2_pe_{b}_{i}", [128, S], F32)) for i in range(2)]
                    pnrow = [ph.enter_context(nc.sbuf_tensor(f"p2_pn_{b}_{i}", [128, S], BF16)) for i in range(4)]
                    dcol = ph.enter_context(nc.sbuf_tensor(f"p2_dc_{b}", [128, 16], F32))
                    ptb = ph.enter_context(nc.sbuf_tensor(f"p2_pt_{b}", [128, 32 * 512], BF16))
                    dstat = ph.enter_context(nc.sbuf_tensor(f"p2_d_{b}", [128, 4], F32))
                    cxb = ph.enter_context(nc.sbuf_tensor(f"p2_cx_{b}", [128, 512], BF16))
                    sps = [ph.enter_context(nc.psum_tensor(f"p2_s{b}_{i}", [128, 512], F32)) for i in range(4)]
                    tps2 = ph.enter_context(nc.psum_tensor(f"p2_tps_{b}", [128, 2048], BF16))
                    ctxps = ph.enter_context(nc.psum_tensor(f"p2_ctx_{b}", [128, 512], F32))

                    hd_ld = {}
                    hd_pe_done = {}
                    prev_head_vnorm = 0
                    for h in range(HPC):
                        @block.sync
                        def _(sync, h=h):
                            if h == 0:
                                sync.wait_ge(stqk.h, p1_stores)
                                sync.wait_ge(pesem.h, pesem.n)
                                sync.wait_ge(ssem.h, ssem.n)
                                sync.wait_ge(vsem.h, vsem.n)
                            if h >= 2:
                                sync.wait_ge(pesem.h, hd_pe_done[h - 2])
                            r0 = (b * HPC + h) * HD
                            ldh[h % 2].tick(sync.dma_start(out=qth[h % 2][:], in_=qt_sp[r0:r0 + 128, :]))
                            ldh[h % 2].tick(sync.dma_start(out=kth[h % 2][:], in_=kt_sp[r0:r0 + 128, :]))
                            vr = (b * HPC + h) * 128
                            ldh[h % 2].tick(sync.dma_start(out=vh[h % 2][:], in_=v_sp[vr:vr + 128, :]))
                        hd_ld[h] = ldh[h % 2].n

                        qq = qth[h % 2]
                        kk = kth[h % 2]
                        vv = vh[h % 2]

                        s_exp_list = []
                        v_norm = {}
                        s_ptev = {}
                        ck_counter = [0]

                        def emit_scores(qt):
                            kext = 128 * (qt + 1)
                            nkc = (kext + 511) // 512
                            for kc in range(nkc):
                                wdt = min(512, kext - kc * 512)
                                ck = ck_counter[0]
                                ck_counter[0] += 1

                                @block.tensor
                                def _(t, qt=qt, kc=kc, wdt=wdt, ck=ck, nkc=nkc):
                                    if (qt, kc) == (0, 0):
                                        t.wait_ge(ldh[h % 2].h, hd_ld[h])
                                    if ck >= 4:
                                        t.wait_ge(ssem.h, s_exp_list[ck - 4])
                                    last = t.matmul(sps[ck % 4][:, 0:wdt],
                                             qq[:, qt * 128:(qt + 1) * 128],
                                             kk[:, kc * 512: kc * 512 + wdt],
                                             start=True, stop=(kc != nkc - 1))
                                    if kc == nkc - 1:
                                        last = t.matmul(sps[ck % 4][:, wdt - 128:wdt],
                                                        ident[:], maskb[:], start=False, stop=True)
                                    pesem.tick(last)
                                pe_s = pesem.n

                                @block.scalar
                                def _(s, qt=qt, kc=kc, wdt=wdt, ck=ck, pe_s=pe_s,
                                      pv=prev_head_vnorm):
                                    s.wait_ge(pesem.h, pe_s)
                                    if kc == 0:
                                        s.wait_ge(vsem.h, v_norm.get(qt - 2, pv))
                                    ssem.tick(s.activation(
                                        perow[qt % 2][:, kc * 512: kc * 512 + wdt],
                                        sps[ck % 4][:, 0:wdt], AF.Exp,
                                        bias=consts[:, C_NEGM:C_NEGM + 1],
                                        scale=consts[:, C_SCALE:C_SCALE + 1],
                                        accum_out=dcol[:, (qt % 2) * 8 + kc:(qt % 2) * 8 + kc + 1]))
                                s_exp_list.append(ssem.n)

                            @block.vector
                            def _(v, qt=qt, kext=kext, nkc=nkc):
                                v.wait_ge(ssem.h, s_exp_list[-1])
                                vsem.tick(v.tensor_reduce(dstat[:, 0:1],
                                                          dcol[:, (qt % 2) * 8:(qt % 2) * 8 + nkc],
                                                          AX.X, ALU.add))
                                v.wait_ge(vsem.h, vsem.n)
                                vsem.tick(v.reciprocal(dstat[:, 1:2], dstat[:, 0:1]))
                                v.wait_ge(vsem.h, vsem.n)
                                vsem.tick(v.tensor_scalar(pnrow[qt % 4][:, 0:kext], perow[qt % 2][:, 0:kext],
                                                          dstat[:, 1:2], None, ALU.mult))
                            v_norm[qt] = vsem.n

                        def emit_tp_pv(qt):
                            ntp = qt + 1
                            ngr = (ntp + 15) // 16
                            for gi in range(ngr):
                                n_in_g = min(16, ntp - gi * 16)

                                @block.tensor
                                def _(t, qt=qt, gi=gi, n_in_g=n_in_g):
                                    t.wait_ge(vsem.h, v_norm[qt])
                                    t.wait_ge(ssem.h, s_ptev_prev[0])
                                    last = None
                                    for i in range(n_in_g):
                                        kt = gi * 16 + i
                                        last = t.transpose(tps2[:, i * 128:(i + 1) * 128],
                                                           pnrow[qt % 4][:, kt * 128:(kt + 1) * 128], ident[:])
                                    pesem.tick(last)
                                pe_t = pesem.n

                                @block.scalar
                                def _(s, qt=qt, gi=gi, n_in_g=n_in_g, pe_t=pe_t):
                                    s.wait_ge(pesem.h, pe_t)
                                    dst = ptb[:].rearrange("p (k q) -> p k q", q=512)[
                                        :, gi * 16: gi * 16 + n_in_g, (qt % 4) * 128:(qt % 4) * 128 + 128]
                                    ssem.tick(s.copy(dst, tps2[:, 0:n_in_g * 128]
                                                     .rearrange("p (k q) -> p k q", q=128)))
                                s_ptev_prev[0] = ssem.n
                            s_ptev[qt] = ssem.n

                            if qt % 4 == 3:
                                qc = qt // 4

                                @block.tensor
                                def _(t, qc=qc, qt=qt):
                                    t.wait_ge(ssem.h, s_ptev[qt])
                                    nkt = 4 * qc + 4
                                    last = None
                                    for kt in range(nkt):
                                        off = max(0, (kt - 4 * qc) * 128)
                                        last = t.matmul(ctxps[:, off:512],
                                                 vv[:, kt * 128:(kt + 1) * 128],
                                                 ptb[:].rearrange("p (k q) -> p k q", q=512)[:, kt, off:512],
                                                 start=(kt == 0), stop=(kt == nkt - 1))
                                    pesem.tick(last)
                                pe_pv = pesem.n

                                @block.scalar
                                def _(s, qc=qc, pe_pv=pe_pv):
                                    s.wait_ge(pesem.h, pe_pv)
                                    if qc > 0:
                                        s.wait_ge(stcx.h, stcx.n)
                                    ssem.tick(s.copy(cxb[:], ctxps[:]))
                                s_cx = ssem.n

                                @block.sync
                                def _(sync, h=h, qc=qc, s_cx=s_cx, b=b):
                                    sync.wait_ge(ssem.h, s_cx)
                                    r0 = b * 512 + h * 128
                                    c0 = qc * 512
                                    stcx.tick(sync.dma_start(out=ctx_c[r0:r0 + 128, c0:c0 + 512],
                                                             in_=cxb[:]))

                        s_ptev_prev = [ssem.n]
                        for qc in range(8):
                            for qt in range(4 * qc, 4 * qc + 4):
                                emit_scores(qt)
                            for qt in range(4 * qc, 4 * qc + 4):
                                emit_tp_pv(qt)

                        hd_pe_done[h] = pesem.n
                        prev_head_vnorm = v_norm[31]

            if nphases >= 3:
                ctx_stores = stcx.n

                @block.gpsimd
                def _(g):
                    g.wait_ge(stcx.h, ctx_stores)
                    cc_ctx.tick(g.collective_compute(
                        "AllGather", ALU.bypass, replica_groups=[list(range(NCORES))],
                        ins=[ctx_c[:].opt()], outs=[ctx_all[:].opt()]))

        if nphases >= 4:
            with ExitStack() as ph:
                xnb4 = ph.enter_context(nc.sbuf_tensor("p4_xn", [128, NDT * 1024], BF16))
                hact = ph.enter_context(nc.sbuf_tensor("p4_ha", [128, 44 * 1024], BF16))
                dq1b = [ph.enter_context(nc.sbuf_tensor(f"p4_d1{i}", [128, D], BF16)) for i in range(2)]
                dq2b = [ph.enter_context(nc.sbuf_tensor(f"p4_d2{i}", [128, D], BF16)) for i in range(2)]
                dq3b = [ph.enter_context(nc.sbuf_tensor(f"p4_d3{i}", [128, 512], BF16)) for i in range(2)]
                sg = ph.enter_context(nc.sbuf_tensor("p4_sg", [128, 1024], F32))
                uf = ph.enter_context(nc.sbuf_tensor("p4_uf", [128, 1024], F32))
                yb = [ph.enter_context(nc.sbuf_tensor(f"p4_y{i}", [128, 512], F32)) for i in range(4)]
                pbank = [ph.enter_context(nc.psum_tensor(f"p4_pb{i}", [128, 512], F32)) for i in range(8)]

                pe_now, v_now, s_now = pesem.n, vsem.n, ssem.n

                @block.sync
                def _(sync):
                    sync.wait_ge(pesem.h, pe_now)
                    sync.wait_ge(vsem.h, v_now)
                    sync.wait_ge(ssem.h, s_now)
                    sync.wait_ge(cc_dq.h, cc_dq.n)
                    for dt in range(NDT):
                        ldxn4.tick(sync.dma_start(out=xnb4[:, dt * 1024:(dt + 1) * 1024],
                                                  in_=xnt_c[dt * 128:(dt + 1) * 128, :]))
                xn_ld = ldxn4.n

                chunks = [[s_ for s_ in REAL_SLOTS if s_ < 44], [s_ for s_ in REAL_SLOTS if s_ >= 44]]
                y_written = {}
                dq_ld = {}
                last_hact_v = 0
                for ci, chunk in enumerate(chunks):
                    s_silu = {}
                    for si, slot in enumerate(chunk):
                        @block.sync
                        def _(sync, ci=ci, si=si, slot=slot):
                            if ci == 0 and si == 0:
                                sync.wait_ge(ldxn4.h, xn_ld)
                            if si >= 2:
                                sync.wait_ge(pesem.h, dq_used[(ci, si - 2)])
                            elif ci > 0:
                                sync.wait_ge(pesem.h, dq_used[(ci - 1, len(chunks[ci - 1]) - 1)])
                            ldd12[si % 2].tick(sync.dma_start(out=dq1b[si % 2][:],
                                                      in_=dq1t_all[slot * 128:(slot + 1) * 128, :]))
                            ldd12[si % 2].tick(sync.dma_start(out=dq2b[si % 2][:],
                                                      in_=dq2t_all[slot * 128:(slot + 1) * 128, :]))
                        dq_ld[(ci, si)] = ldd12[si % 2].n
                        if ci == 0 and si == 0:
                            dq_used = {}

                        @block.tensor
                        def _(t, ci=ci, si=si):
                            t.wait_ge(ldd12[si % 2].h, dq_ld[(ci, si)])
                            if s_silu.get(si - 1) or (ci > 0 and si == 0):
                                t.wait_ge(ssem.h, ssem.n)
                            last = None
                            for dt in range(NDT):
                                st_ = (dt == 0); sp = (dt == NDT - 1)
                                l1 = dq1b[si % 2][:, dt * 128:(dt + 1) * 128]
                                l2 = dq2b[si % 2][:, dt * 128:(dt + 1) * 128]
                                t.matmul(pbank[0][:], l1, xnb4[:, dt * 1024: dt * 1024 + 512], start=st_, stop=sp)
                                t.matmul(pbank[1][:], l1, xnb4[:, dt * 1024 + 512: dt * 1024 + 1024], start=st_, stop=sp)
                                t.matmul(pbank[2][:], l2, xnb4[:, dt * 1024: dt * 1024 + 512], start=st_, stop=sp)
                                last = t.matmul(pbank[3][:], l2, xnb4[:, dt * 1024 + 512: dt * 1024 + 1024], start=st_, stop=sp)
                            pesem.tick(last)
                        pe_gu = pesem.n
                        dq_used[(ci, si)] = pe_gu

                        @block.scalar
                        def _(s, ci=ci, si=si, pe_gu=pe_gu):
                            s.wait_ge(pesem.h, pe_gu)
                            if last_hact_v:
                                s.wait_ge(vsem.h, last_hact_v)
                            s.activation(sg[:, 0:512], pbank[0][:], AF.Silu)
                            s.activation(sg[:, 512:1024], pbank[1][:], AF.Silu)
                            s.copy(uf[:, 0:512], pbank[2][:])
                            ssem.tick(s.copy(uf[:, 512:1024], pbank[3][:]))
                        s_silu[si] = ssem.n

                        @block.vector
                        def _(v, ci=ci, si=si):
                            v.wait_ge(ssem.h, s_silu[si])
                            vsem.tick(v.tensor_tensor(hact[:, si * 1024:(si + 1) * 1024],
                                                      sg[:], uf[:], ALU.mult))
                        last_hact_v = vsem.n

                    nsl = len(chunk)
                    for c in range(8):
                        d3_ld = {}
                        for si, slot in enumerate(chunk):
                            @block.sync
                            def _(sync, ci=ci, c=c, si=si, slot=slot):
                                key = None
                                if si >= 2:
                                    key = (ci, c, si - 2)
                                elif c > 0:
                                    key = (ci, c - 1, nsl - 1)
                                elif ci > 0:
                                    key = (ci - 1, 7, len(chunks[ci - 1]) - 1)
                                if key is not None and key in d3_used:
                                    sync.wait_ge(pesem.h, d3_used[key])
                                ldd3[si % 2].tick(sync.dma_start(
                                    out=dq3b[si % 2][:],
                                    in_=dq3t_all[slot * 128:(slot + 1) * 128, c * 512:(c + 1) * 512]))
                            d3_ld[si] = ldd3[si % 2].n
                            if (ci, c, si) == (0, 0, 0):
                                d3_used = {}

                            @block.tensor
                            def _(t, ci=ci, c=c, si=si, nsl=nsl):
                                t.wait_ge(ldd3[si % 2].h, d3_ld[si])
                                if si == 0:
                                    t.wait_ge(vsem.h, last_hact_v)
                                    if (ci, c) != (0, 0):
                                        t.wait_ge(ssem.h, ssem.n)
                                last = None
                                for tt in range(NTT):
                                    last = t.matmul(pbank[tt][:],
                                                    hact[:, si * 1024 + tt * 128: si * 1024 + (tt + 1) * 128],
                                                    dq3b[si % 2][:],
                                                    start=(si == 0), stop=(si == nsl - 1))
                                pesem.tick(last)
                            d3_used[(ci, c, si)] = pesem.n
                        pe_dn = pesem.n

                        for tt in range(NTT):
                            @block.scalar
                            def _(s, ci=ci, c=c, tt=tt, pe_dn=pe_dn):
                                s.wait_ge(pesem.h, pe_dn)
                                if sty.n:
                                    s.wait_ge(sty.h, sty.n)
                                if gy.n:
                                    s.wait_ge(gy.h, gy.n)
                                ssem.tick(s.copy(yb[tt % 2][:], pbank[tt][:]))
                            s_y = ssem.n

                            if ci == 0:
                                @block.sync
                                def _(sync, c=c, tt=tt, s_y=s_y):
                                    sync.wait_ge(ssem.h, s_y)
                                    sty.tick(sync.dma_start(
                                        out=y_hbm[tt * 128:(tt + 1) * 128, c * 512:(c + 1) * 512],
                                        in_=yb[tt % 2][:]))
                                y_written[(ci, c, tt)] = sty.n
                            else:
                                @block.gpsimd
                                def _(g, c=c, tt=tt, s_y=s_y, sty_all=None):
                                    g.wait_ge(ssem.h, s_y)
                                    g.wait_ge(sty.h, sty.n)
                                    gy.tick(g.dma_start(
                                        out=y_hbm[tt * 128:(tt + 1) * 128, c * 512:(c + 1) * 512],
                                        in_=yb[tt % 2][:], accum_op=ALU.add))
                                y_written[(ci, c, tt)] = gy.n

        if nphases >= 5:
            with ExitStack() as ph:
                ctxme = ph.enter_context(nc.sbuf_tensor("p3_cx", [128, NDT * 1024], BF16))
                cand = [ph.enter_context(nc.sbuf_tensor(f"p3_cd{i}", [128, 1024], BF16)) for i in range(2)]
                ctmp = ph.enter_context(nc.sbuf_tensor("p3_ct", [128, 1024], BF16))
                cmsk = ph.enter_context(nc.sbuf_tensor("p3_cm", [128, 8], F32))
                wob = [ph.enter_context(nc.sbuf_tensor(f"p3_wo{i}", [128, 512], BF16)) for i in range(2)]
                gbuf = [ph.enter_context(nc.sbuf_tensor(f"p3_g{i}", [128, 512], F32)) for i in range(4)]
                ybuf = [ph.enter_context(nc.sbuf_tensor(f"p3_y{i}", [128, 512], F32)) for i in range(4)]
                xbuf = [ph.enter_context(nc.sbuf_tensor(f"p3_x{i}", [128, 512], F32)) for i in range(4)]
                sf = ph.enter_context(nc.sbuf_tensor("p3_sf", [128, 512], F32))
                ob = [ph.enter_context(nc.sbuf_tensor(f"p3_o{i}", [128, 512], F32)) for i in range(4)]
                spank = [ph.enter_context(nc.psum_tensor(f"p3_pb{i}", [128, 512], F32)) for i in range(8)]

                pe_now, v_now, s_now = pesem.n, vsem.n, ssem.n

                @block.sync
                def _(sync):
                    sync.wait_ge(pesem.h, pe_now)
                    sync.wait_ge(vsem.h, v_now)
                    sync.wait_ge(ssem.h, s_now)
                    sync.wait_ge(cc_ctx.h, cc_ctx.n)
                    ldcd[0].tick(sync.dma_start(out=cmsk[:], in_=cmask_in[:]))

                v_asm = {}
                for dt in range(NDT):
                    cd_ld = {}
                    for j in range(NCORES):
                        @block.sync
                        def _(sync, dt=dt, j=j):
                            k = dt * 8 + j
                            if k >= 2:
                                pk = ((k - 2) // 8, (k - 2) % 8)
                                sync.wait_ge(vsem.h, v_asm[pk])
                            row0 = (dt // 4) * 1024 + (j // 4) * 512 + (dt % 4) * 128
                            col0 = (j % 4) * 1024
                            ldcd[j % 2].tick(sync.dma_start(out=cand[j % 2][:],
                                                      in_=ctx_all[row0:row0 + 128, col0:col0 + 1024]))
                        cd_ld[j] = ldcd[j % 2].n

                        @block.vector
                        def _(v, dt=dt, j=j):
                            v.wait_ge(ldcd[j % 2].h, cd_ld[j])
                            if j == 0:
                                vsem.tick(v.tensor_scalar(ctxme[:, dt * 1024:(dt + 1) * 1024],
                                                          cand[0][:], cmsk[:, 0:1], None, ALU.mult))
                            else:
                                v.tensor_scalar(ctmp[:], cand[j % 2][:], cmsk[:, j:j + 1], None, ALU.mult)
                                vsem.tick(v.tensor_tensor(ctxme[:, dt * 1024:(dt + 1) * 1024],
                                                          ctxme[:, dt * 1024:(dt + 1) * 1024],
                                                          ctmp[:], ALU.add))
                        v_asm[(dt, j)] = vsem.n
                asm_done = vsem.n

                wo_ld = {}
                out_written = {}
                for c in range(8):
                    for dt in range(NDT):
                        @block.gpsimd
                        def _(g, c=c, dt=dt):
                            k = c * NDT + dt
                            if k == 0:
                                g.wait_ge(vsem.h, asm_done)
                            if k >= 2:
                                pk = ((k - 2) // NDT, (k - 2) % NDT)
                                g.wait_ge(pesem.h, wo_used[pk])
                            gwo[dt % 2].tick(g.dma_start(out=wob[dt % 2][:],
                                                 in_=wo_full[dt * 128:(dt + 1) * 128, c * 512:(c + 1) * 512]))
                        wo_ld[(c, dt)] = gwo[dt % 2].n
                        if (c, dt) == (0, 0):
                            wo_used = {}

                        @block.tensor
                        def _(t, c=c, dt=dt):
                            t.wait_ge(gwo[dt % 2].h, wo_ld[(c, dt)])
                            if dt == 0 and c > 0:
                                t.wait_ge(ssem.h, ssem.n)
                            last = None
                            for tt in range(NTT):
                                last = t.matmul(spank[tt][:],
                                                ctxme[:, dt * 1024 + tt * 128: dt * 1024 + (tt + 1) * 128],
                                                wob[dt % 2][:],
                                                start=(dt == 0), stop=(dt == NDT - 1))
                            pesem.tick(last)
                        wo_used[(c, dt)] = pesem.n
                    pe_sp = pesem.n

                    for tt in range(NTT):
                        @block.sync
                        def _(sync, c=c, tt=tt):
                            k = c * NTT + tt
                            if k >= 2:
                                pk = ((k - 2) // NTT, (k - 2) % NTT)
                                sync.wait_ge(vsem.h, out_comb[pk])
                            ldyx[tt % 2].tick(sync.dma_start(out=ybuf[tt % 2][:],
                                                     in_=y_hbm[tt * 128:(tt + 1) * 128, c * 512:(c + 1) * 512]))
                            ldyx[tt % 2].tick(sync.dma_start(out=xbuf[tt % 2][:],
                                                     in_=x_own[tt * 128:(tt + 1) * 128, c * 512:(c + 1) * 512]))
                        yx_ld = ldyx[tt % 2].n
                        if (c, tt) == (0, 0):
                            out_comb = {}

                        @block.scalar
                        def _(s, c=c, tt=tt, yx_ld=yx_ld, pe_sp=pe_sp):
                            s.wait_ge(pesem.h, pe_sp)
                            s.wait_ge(ldyx[tt % 2].h, yx_ld)
                            if y_written:
                                s.wait_ge(gy.h, max(v for k, v in y_written.items() if k[0] == 1))
                            if (c, tt) != (0, 0):
                                s.wait_ge(vsem.h, vsem.n)
                            ssem.tick(s.activation(gbuf[tt % 2][:], ybuf[tt % 2][:], AF.Silu))
                            ssem.tick(s.copy(sf[:], spank[tt][:]))
                        s_fin = ssem.n

                        @block.vector
                        def _(v, c=c, tt=tt, s_fin=s_fin):
                            v.wait_ge(ssem.h, s_fin)
                            v.tensor_tensor(ob[tt % 2][:], sf[:], gbuf[tt % 2][:], ALU.mult)
                            vsem.tick(v.tensor_tensor(ob[tt % 2][:], ob[tt % 2][:], xbuf[tt % 2][:], ALU.add))
                        out_comb[(c, tt)] = vsem.n

                        @block.sync
                        def _(sync, c=c, tt=tt):
                            sync.wait_ge(vsem.h, out_comb[(c, tt)])
                            outsem.tick(sync.dma_start(
                                out=out[tt * 128:(tt + 1) * 128, c * 512:(c + 1) * 512],
                                in_=ob[tt % 2][:]))

        @block.sync
        def _(sync):
            for ctr in (cc_xnt, cc_dq, cc_ctx):
                if ctr.n:
                    sync.wait_ge(ctr.h, ctr.n)
            if nphases < 5:
                for rr in range(TC // 128):
                    stw.tick(sync.dma_start(out=out[rr * 128:(rr + 1) * 128, 0:NCONST], in_=consts[:]))
            for name, (tp, src, shape) in tap_out.items():
                rows = shape[0]
                step = max(128, rows // 8)
                r = 0
                while r < rows:
                    n = min(step, rows - r)
                    stw.tick(sync.dma_start(out=tp[r:r + n, :], in_=src[r:r + n, :]))
                    r += n
            for ctr in (stw, gld, ld, stxnt, stdq, stqk, stcx, sty, gy, outsem,
                        ldx[0], ldx[1], ldh[0], ldh[1], ldd12[0], ldd12[1],
                        ldd3[0], ldd3[1], ldcd[0], ldcd[1], gwo[0], gwo[1],
                        ldyx[0], ldyx[1], gwc, ldxc, ldw, ldxn4, ldyx[0]):
                if ctr.n:
                    sync.wait_ge(ctr.h, ctr.n)

    return nc, [n for n in tap_out]


def _shard_inputs(L0_attn_hidden_states, L0_attn_wq, L0_attn_wk, L0_attn_wv,
                  L0_attn_wo, L0_ffn_w1, L0_ffn_w2, L0_ffn_w3):
    x = np.ascontiguousarray(np.asarray(L0_attn_hidden_states, dtype=np.float32).reshape(B * S, D))
    wq = np.asarray(L0_attn_wq, dtype=np.float32)
    wk = np.asarray(L0_attn_wk, dtype=np.float32)
    wv = np.asarray(L0_attn_wv, dtype=np.float32)
    wo = np.ascontiguousarray(np.asarray(L0_attn_wo, dtype=np.float32))
    w1 = np.asarray(L0_ffn_w1, dtype=np.float32)
    w2 = np.asarray(L0_ffn_w2, dtype=np.float32)
    w3 = np.asarray(L0_ffn_w3, dtype=np.float32)

    cos_np, sin_np = _rope_tables()
    cb = _const_bank()
    ident_np = np.eye(128, dtype=np.float32)
    mask_np = _mask_tile()

    in_maps = []
    for c in range(NCORES):
        t0, t1 = CORE_TILE_START[c] * 128, (CORE_TILE_START[c] + CORE_TILES[c]) * 128
        npad = SPC * 128 - (t1 - t0)
        w1s = np.ascontiguousarray(w1[t0:t1, :])
        w2s = np.ascontiguousarray(w2[t0:t1, :])
        w3s = np.ascontiguousarray(w3[:, t0:t1])
        if npad:
            w1s = np.concatenate([w1s, w1s[:npad]], axis=0)
            w2s = np.concatenate([w2s, w2s[:npad]], axis=0)
            w3s = np.concatenate([w3s, w3s[:, :npad]], axis=1)
        in_maps.append({
            "x_own": np.ascontiguousarray(x[c * TC:(c + 1) * TC, :]),
            "wq_sl": np.ascontiguousarray(wq[:, c * 512:(c + 1) * 512]),
            "wk_sl": np.ascontiguousarray(wk[:, c * 512:(c + 1) * 512]),
            "wv_sl": np.ascontiguousarray(wv[:, c * 512:(c + 1) * 512]),
            "wo_full": wo,
            "w1_sh": np.ascontiguousarray(w1s),
            "w2_sh": np.ascontiguousarray(w2s),
            "w3_sh": np.ascontiguousarray(w3s),
            "cos_t": cos_np, "sin_t": sin_np, "cbank": cb,
            "ident_in": ident_np, "mask_in": mask_np,
            "cmask_in": np.eye(8, dtype=np.float32)[c][None, :].repeat(128, 0),
        })
    return in_maps


def kernel(**inputs):
    in_maps = _shard_inputs(**inputs)
    nc, _ = build_program()
    res = run_bass_kernel_spmd(nc, in_maps, core_ids=list(range(NCORES)))
    outs = [np.asarray(res.results[c]["out"]) for c in range(NCORES)]
    return np.concatenate(outs, axis=0).reshape(B, S, D)


# version 09 (final)
# speedup vs baseline: 1.0268x; 1.0268x over previous
import numpy as np
import ml_dtypes
from contextlib import ExitStack

import concourse.bass as bass
import concourse.mybir as mybir
from concourse.bass_utils import run_bass_kernel_spmd

F32 = mybir.dt.float32
BF16 = mybir.dt.bfloat16
AF = mybir.ActivationFunctionType
ALU = mybir.AluOpType
AX = mybir.AxisListType

NCORES = 8
B = 2
S = 4096
D = 4096
HD = 128
HH = 64
HF = 11008
TC = 1024
NDT = D // 128
NTT = TC // 128
NHT = HF // 128
SPC = 11
NSLOT = NCORES * SPC
HPC = 4

CORE_TILES = [11, 11, 11, 11, 11, 11, 10, 10]
CORE_TILE_START = [0, 11, 22, 33, 44, 55, 66, 76]
REAL_SLOTS = []
for _r in range(NCORES):
    for _k in range(CORE_TILES[_r]):
        REAL_SLOTS.append(11 * _r + _k)
assert len(REAL_SLOTS) == NHT

NF4_CODE = np.array([-1.0, -0.6961928009986877, -0.5250730514526367,
                     -0.39491748809814453, -0.28444138169288635, -0.18477343022823334,
                     -0.09105003625154495, 0.0, 0.07958029955625534, 0.16093020141124725,
                     0.24611230194568634, 0.33791524171829224, 0.44070982933044434,
                     0.5626170039176941, 0.7229568362236023, 1.0], dtype=np.float32)
NF4_BOUNDS = ((NF4_CODE[:-1] + NF4_CODE[1:]) * np.float32(0.5)).astype(np.float32)
NF4_GAMMA = ((NF4_CODE[1:] - NF4_CODE[:-1]) * np.float32(0.5)).astype(np.float32)
BLK = 64
NSC = 11

C_EPS, C_MHALF, C_INVD, C_SCALE, C_ONE = 0, 1, 2, 3, 4
C_NB0 = 5
C_G0 = 20
C_NEGM = 35
NCONST = 36


def _rope_tables():
    try:
        import jax
        import jax.numpy as jnp
        with jax.default_device(jax.devices("cpu")[0]):
            inv_freq = 1.0 / 10000.0 ** (jnp.arange(0, HD, 2, dtype=jnp.float32) / HD)
            pos = jnp.arange(S, dtype=jnp.float32)
            freqs = pos[:, None] * inv_freq[None, :]
            emb = jnp.concatenate([freqs, freqs], axis=-1)
            cos_np = np.asarray(jax.device_get(jnp.cos(emb)), dtype=np.float32)
            sin_np = np.asarray(jax.device_get(jnp.sin(emb)), dtype=np.float32)
        return cos_np, sin_np
    except Exception:
        inv_freq = (np.float32(1.0) /
                    np.float32(10000.0) ** (np.arange(0, HD, 2, dtype=np.float32) / np.float32(HD)))
        pos = np.arange(S, dtype=np.float32)
        freqs = (pos[:, None] * inv_freq[None, :]).astype(np.float32)
        emb = np.concatenate([freqs, freqs], axis=-1).astype(np.float32)
        return np.cos(emb).astype(np.float32), np.sin(emb).astype(np.float32)


def _const_bank():
    cb = np.zeros((128, NCONST), dtype=np.float32)
    cb[:, C_EPS] = np.float32(1e-5)
    cb[:, C_MHALF] = np.float32(-0.5)
    cb[:, C_INVD] = np.float32(1.0 / 4096.0)
    cb[:, C_SCALE] = np.float32(1.0) / np.float32(np.sqrt(np.float32(HD)))
    cb[:, C_ONE] = np.float32(1.0)
    for j in range(15):
        cb[:, C_NB0 + j] = -NF4_BOUNDS[j]
        cb[:, C_G0 + j] = NF4_GAMMA[j]
    cb[:, C_NEGM] = np.float32(-10.0)
    return cb


def _mask_tile():
    m = np.zeros((128, 128), dtype=np.float32)
    m[np.triu_indices(128, 1)] = np.float32(-1.0e30)
    return m


class Ctr:
    def __init__(self, stack, nc, name, step=1):
        self.h = stack.enter_context(nc.semaphore(name))
        self.n = 0
        self.step = step

    def tick(self, instr):
        self.n += self.step
        instr.then_inc(self.h, self.step)
        return self.n

    def wait(self, eng, val=None):
        eng.wait_ge(self.h, self.n if val is None else val)


def build_program(nphases=99, taps=(), detect_races=True, skip_ag=False):
    P1LVL = 99
    nc = bass.Bass("TRN2", target_bir_lowering=False, detect_race_conditions=detect_races)
    taps = list(taps)

    x_own = nc.dram_tensor("x_own", [TC, D], F32, kind="ExternalInput").ap()
    wq_sl = nc.dram_tensor("wq_sl", [D, 512], F32, kind="ExternalInput").ap()
    wk_sl = nc.dram_tensor("wk_sl", [D, 512], F32, kind="ExternalInput").ap()
    wv_sl = nc.dram_tensor("wv_sl", [D, 512], F32, kind="ExternalInput").ap()
    wo_full = nc.dram_tensor("wo_full", [D, D], F32, kind="ExternalInput").ap()
    w1_sh = nc.dram_tensor("w1_sh", [SPC * 128, D], F32, kind="ExternalInput").ap()
    w2_sh = nc.dram_tensor("w2_sh", [SPC * 128, D], F32, kind="ExternalInput").ap()
    w3_sh = nc.dram_tensor("w3_sh", [D, SPC * 128], F32, kind="ExternalInput").ap()
    cos_t = nc.dram_tensor("cos_t", [S, HD], F32, kind="ExternalInput").ap()
    sin_t = nc.dram_tensor("sin_t", [S, HD], F32, kind="ExternalInput").ap()
    cbank = nc.dram_tensor("cbank", [128, NCONST], F32, kind="ExternalInput").ap()
    ident_in = nc.dram_tensor("ident_in", [128, 128], F32, kind="ExternalInput").ap()
    mask_in = nc.dram_tensor("mask_in", [128, 128], F32, kind="ExternalInput").ap()
    cmask_in = nc.dram_tensor("cmask_in", [128, 8], F32, kind="ExternalInput").ap()

    out = nc.dram_tensor("out", [TC, D], F32, kind="ExternalOutput").ap()

    xnt_c = nc.dram_tensor("xnt_c", [D, TC], BF16, kind="Internal").ap()
    xnt_all = nc.dram_tensor("xnt_all", [NCORES * D, TC], BF16, kind="Internal").ap()
    dq1t_c = nc.dram_tensor("dq1t_c", [SPC * 128, D], BF16, kind="Internal").ap()
    dq2t_c = nc.dram_tensor("dq2t_c", [SPC * 128, D], BF16, kind="Internal").ap()
    dq3t_c = nc.dram_tensor("dq3t_c", [SPC * 128, D], BF16, kind="Internal").ap()
    dq1t_all = nc.dram_tensor("dq1t_all", [NSLOT * 128, D], BF16, kind="Internal").ap()
    dq2t_all = nc.dram_tensor("dq2t_all", [NSLOT * 128, D], BF16, kind="Internal").ap()
    dq3t_all = nc.dram_tensor("dq3t_all", [NSLOT * 128, D], BF16, kind="Internal").ap()
    qt_sp = nc.dram_tensor("qt_sp", [B * HPC * HD, S], BF16, kind="Internal").ap()
    kt_sp = nc.dram_tensor("kt_sp", [B * HPC * HD, S], BF16, kind="Internal").ap()
    v_sp = nc.dram_tensor("v_sp", [B * HPC * 128, S], BF16, kind="Internal").ap()
    ctx_c = nc.dram_tensor("ctx_c", [B * 512, S], BF16, kind="Internal").ap()
    ctx_all = nc.dram_tensor("ctx_all", [NCORES * B * 512, S], BF16, kind="Internal").ap()
    y_hbm = nc.dram_tensor("y_hbm", [TC, D], F32, kind="Internal").ap()

    TAP_SRC = {
        "xnt_all": (xnt_all, [NCORES * D, TC], BF16),
        "dq1t_all": (dq1t_all, [NSLOT * 128, D], BF16),
        "dq2t_all": (dq2t_all, [NSLOT * 128, D], BF16),
        "dq3t_all": (dq3t_all, [NSLOT * 128, D], BF16),
        "qt_sp": (qt_sp, [B * HPC * HD, S], BF16),
        "kt_sp": (kt_sp, [B * HPC * HD, S], BF16),
        "v_sp": (v_sp, [B * HPC * 128, S], BF16),
        "ctx_all": (ctx_all, [NCORES * B * 512, S], BF16),
        "y_hbm": (y_hbm, [TC, D], F32),
    }
    tap_out = {}
    for name in taps:
        src, shape, dt = TAP_SRC[name]
        tap_out[name] = (nc.dram_tensor("tap_" + name, shape, dt, kind="ExternalOutput").ap(), src, shape)

    st = ExitStack()
    with st:
        consts = st.enter_context(nc.sbuf_tensor("consts", [128, NCONST], F32))
        ident_f = st.enter_context(nc.sbuf_tensor("ident_f", [128, 128], F32))
        ident = st.enter_context(nc.sbuf_tensor("ident", [128, 128], BF16))
        maskt = st.enter_context(nc.sbuf_tensor("maskt", [128, 128], F32))

        ld = Ctr(st, nc, "ld", 16)
        stw = Ctr(st, nc, "stw", 16)
        gld = Ctr(st, nc, "gld", 16)
        ldx = [Ctr(st, nc, "ldx0", 16), Ctr(st, nc, "ldx1", 16)]
        stxnt = Ctr(st, nc, "stxnt", 16)
        ldw = Ctr(st, nc, "ldw", 16)
        stdq = Ctr(st, nc, "stdq", 16)
        gsem = Ctr(st, nc, "gsem")
        gwc = Ctr(st, nc, "gwc", 16)
        ldxc = Ctr(st, nc, "ldxc", 16)
        stqk = Ctr(st, nc, "stqk", 16)
        ldh = [Ctr(st, nc, "ldh0", 16), Ctr(st, nc, "ldh1", 16)]
        stcx = Ctr(st, nc, "stcx", 16)
        ldxn4 = Ctr(st, nc, "ldxn4", 16)
        ldd12 = [Ctr(st, nc, "ldd12a", 16), Ctr(st, nc, "ldd12b", 16)]
        ldd3 = [Ctr(st, nc, "ldd3a", 16), Ctr(st, nc, "ldd3b", 16)]
        sty = Ctr(st, nc, "sty", 16)
        gy = Ctr(st, nc, "gy", 16)
        ldcd = [Ctr(st, nc, "ldcd0", 16), Ctr(st, nc, "ldcd1", 16)]
        gwo = [Ctr(st, nc, "gwo0", 16), Ctr(st, nc, "gwo1", 16)]
        ldyx = [Ctr(st, nc, "ldyx0", 16), Ctr(st, nc, "ldyx1", 16)]
        vsem = Ctr(st, nc, "vsem")
        ssem = Ctr(st, nc, "ssem")
        pesem = Ctr(st, nc, "pesem")
        cc_xnt = Ctr(st, nc, "cc_xnt")
        cc_dq = Ctr(st, nc, "cc_dq")
        cc_ctx = Ctr(st, nc, "cc_ctx")
        outsem = Ctr(st, nc, "outsem", 16)

        block = st.enter_context(nc.Block())

        @block.sync
        def _(sync):
            sync.dma_start(out=consts[:], in_=cbank[:]).then_inc(ld.h, 16)
            sync.dma_start(out=ident_f[:], in_=ident_in[:]).then_inc(ld.h, 16)
            sync.dma_start(out=maskt[:], in_=mask_in[:]).then_inc(ld.h, 16)
        ld.n += 48

        maskb = st.enter_context(nc.sbuf_tensor("maskb", [128, 128], BF16))

        @block.vector
        def _(v):
            v.wait_ge(ld.h, ld.n)
            v.tensor_copy(ident[:], ident_f[:])
            vsem.tick(v.tensor_copy(maskb[:], maskt[:]))
        ident_ready = vsem.n
        consts_ready = ld.n

        @block.tensor
        def _(t):
            t.wait_ge(vsem.h, ident_ready)

        with ExitStack() as ph:
            xs = [ph.enter_context(nc.sbuf_tensor(f"p0_x{i}", [128, D], F32)) for i in range(2)]
            xc = ph.enter_context(nc.sbuf_tensor("p0_xc", [128, D], F32))
            sq = ph.enter_context(nc.sbuf_tensor("p0_sq", [128, D], F32))
            stats = ph.enter_context(nc.sbuf_tensor("p0_st", [128, 8], F32))
            xnb = ph.enter_context(nc.sbuf_tensor("p0_xnb", [128, D], BF16))
            ev = ph.enter_context(nc.sbuf_tensor("p0_ev", [128, 1024], BF16))
            tps = ph.enter_context(nc.psum_tensor("p0_tps", [128, 1024], BF16))

            v_xc_done = []
            v_evac = {}
            pe_tp = {}
            s_rstd = []
            ev_st = {}

            for tt in range(NTT):
                @block.sync
                def _(sync, tt=tt):
                    if tt == 0:
                        sync.wait_ge(ld.h, consts_ready)
                    if tt >= 2:
                        sync.wait_ge(vsem.h, v_xc_done[tt - 2])
                    ldx[tt % 2].tick(sync.dma_start(out=xs[tt % 2][:], in_=x_own[tt * 128:(tt + 1) * 128, :]))
                x_ld = ldx[tt % 2].n

                @block.vector
                def _(v, tt=tt, x_ld=x_ld):
                    v.wait_ge(ldx[tt % 2].h, x_ld)
                    if tt > 0:
                        v.wait_ge(ssem.h, s_rstd[tt - 1])
                    vsem.tick(v.tensor_reduce(stats[:, 0:1], xs[tt % 2][:], AX.X, ALU.add))
                    v.wait_ge(vsem.h, vsem.n)
                    vsem.tick(v.tensor_scalar(stats[:, 1:2], stats[:, 0:1], consts[:, C_INVD:C_INVD + 1], None, ALU.mult))
                    v.wait_ge(vsem.h, vsem.n)
                    vsem.tick(v.tensor_scalar(xc[:], xs[tt % 2][:], stats[:, 1:2], None, ALU.subtract))
                v_xc_done.append(vsem.n)

                @block.scalar
                def _(s, tt=tt):
                    s.wait_ge(vsem.h, v_xc_done[tt])
                    ssem.tick(s.activation(sq[:], xc[:], AF.Square, accum_out=stats[:, 2:3]))
                    s.wait_ge(ssem.h, ssem.n)
                    ssem.tick(s.mul(stats[:, 3:4], stats[:, 2:3], consts[:, C_INVD:C_INVD + 1]))
                    s.wait_ge(ssem.h, ssem.n)
                    ssem.tick(s.activation(stats[:, 4:5], stats[:, 3:4], AF.Ln, bias=consts[:, C_EPS:C_EPS + 1]))
                    s.wait_ge(ssem.h, ssem.n)
                    ssem.tick(s.activation(stats[:, 5:6], stats[:, 4:5], AF.Exp,
                                           scale=consts[:, C_MHALF:C_MHALF + 1]))
                s_rstd.append(ssem.n)

                @block.vector
                def _(v, tt=tt):
                    v.wait_ge(ssem.h, s_rstd[tt])
                    if tt > 0:
                        v.wait_ge(pesem.h, pe_tp[(tt - 1, 3)])
                    vsem.tick(v.tensor_scalar(xnb[:], xc[:], stats[:, 5:6], None, ALU.mult))
                v_xnb = vsem.n

                for g in range(4):
                    @block.tensor
                    def _(t, tt=tt, g=g, v_xnb=v_xnb):
                        t.wait_ge(vsem.h, v_xnb)
                        if (tt, g) != (0, 0):
                            prev = (tt, g - 1) if g > 0 else (tt - 1, 3)
                            t.wait_ge(ssem.h, v_evac[prev])
                        last = None
                        for i in range(8):
                            dt = g * 8 + i
                            last = t.transpose(tps[:, i * 128:(i + 1) * 128],
                                               xnb[:, dt * 128:(dt + 1) * 128], ident[:])
                        pesem.tick(last)
                    pe_tp[(tt, g)] = pesem.n

                    @block.scalar
                    def _(sc, tt=tt, g=g):
                        sc.wait_ge(pesem.h, pe_tp[(tt, g)])
                        if (tt, g) != (0, 0):
                            prev = (tt, g - 1) if g > 0 else (tt - 1, 3)
                            sc.wait_ge(stxnt.h, ev_st[prev])
                        ssem.tick(sc.copy(ev[:], tps[:]))
                    v_evac[(tt, g)] = ssem.n

                    @block.sync
                    def _(sync, tt=tt, g=g):
                        sync.wait_ge(ssem.h, v_evac[(tt, g)])
                        for i in range(8):
                            dt = g * 8 + i
                            stxnt.tick(sync.dma_start(
                                out=xnt_c[dt * 128:(dt + 1) * 128, tt * 128:(tt + 1) * 128],
                                in_=ev[:, i * 128:(i + 1) * 128]))
                    ev_st[(tt, g)] = stxnt.n

            xnt_stores = stxnt.n

            if not skip_ag:
                @block.gpsimd
                def _(g):
                    g.wait_ge(stxnt.h, xnt_stores)
                    cc_xnt.tick(g.collective_compute(
                        "AllGather", ALU.bypass,
                        replica_groups=[list(range(NCORES))],
                        ins=[xnt_c[:].opt()], outs=[xnt_all[:].opt()]))

        if nphases >= 1:
            with ExitStack() as ph:
                wrow = [ph.enter_context(nc.sbuf_tensor(f"dq_w{i}", [128, D], F32)) for i in range(2)]
                nrm = ph.enter_context(nc.sbuf_tensor("dq_n", [128, D], F32))
                term = [ph.enter_context(nc.sbuf_tensor(f"dq_t{i}", [128, D], F32)) for i in range(3)]
                vacc = ph.enter_context(nc.sbuf_tensor("dq_va", [128, D], F32))
                acc = ph.enter_context(nc.sbuf_tensor("dq_a", [128, D], F32))
                dqb = ph.enter_context(nc.sbuf_tensor("dq_b", [128, D], BF16))
                am = ph.enter_context(nc.sbuf_tensor("dq_am", [128, D // BLK], F32))
                am2 = ph.enter_context(nc.sbuf_tensor("dq_am2", [128, D // BLK], F32))
                ev = ph.enter_context(nc.sbuf_tensor("dq_ev", [128, 1024], BF16))
                tps = ph.enter_context(nc.psum_tensor("dq_tps", [128, 1024], BF16))

                state = {"v_nrm": 0, "pe_last": 0, "s_last_sign": 0,
                         "dq_ev_st": 0, "v_ev_prev": 0}

                def dq_tile(src_ap, ncols, writes, tile_id):
                    nblk = ncols // BLK
                    ngrp = ncols // 128
                    r3 = lambda ap: ap.rearrange("p (n b) -> p n b", b=BLK)

                    wr = wrow[tile_id % 2]

                    @block.sync
                    def _(sync):
                        if tile_id > 1:
                            sync.wait_ge(vsem.h, state.get(f"v_nrm_{tile_id % 2}", 0))
                        ldw.tick(sync.dma_start(out=wr[:, 0:ncols], in_=src_ap))
                    w_ld = ldw.n

                    @block.vector
                    def _(v):
                        v.wait_ge(ldw.h, w_ld)
                        if tile_id > 0:
                            v.wait_ge(pesem.h, state["pe_last"])
                            v.wait_ge(ssem.h, state["s_last_sign"])
                        vsem.tick(v.tensor_reduce(am[:, 0:nblk], r3(wr[:, 0:ncols]), AX.X, ALU.max,
                                                   apply_absolute_value=True))
                        v.wait_ge(vsem.h, vsem.n)
                        vsem.tick(v.tensor_scalar(am2[:, 0:nblk], am[:, 0:nblk], 0.0, None, ALU.is_equal))
                        v.wait_ge(vsem.h, vsem.n)
                        vsem.tick(v.tensor_tensor(am2[:, 0:nblk], am[:, 0:nblk], am2[:, 0:nblk], ALU.add))
                        v.wait_ge(vsem.h, vsem.n)
                        vsem.tick(v.reciprocal(am2[:, 0:nblk], am2[:, 0:nblk]))
                        v.wait_ge(vsem.h, vsem.n)
                        vsem.tick(v.tensor_tensor(
                            r3(nrm[:, 0:ncols]), r3(wr[:, 0:ncols]),
                            am2[:, 0:nblk].rearrange("p (n o) -> p n o", o=1).broadcast_to([128, nblk, BLK]),
                            ALU.mult))
                    state["v_nrm"] = vsem.n
                    state[f"v_nrm_{tile_id % 2}"] = vsem.n
                    v_nrm = vsem.n

                    NGP = 0
                    v_term = {}
                    g_add = {}
                    for j in range(15):
                        tb = term[j % 3]

                        @block.vector
                        def _(v, j=j, tb=tb):
                            if j == 0 and gsem.n:
                                v.wait_ge(gsem.h, gsem.n)
                            if j >= 3 and (j - 3) < NGP:
                                v.wait_ge(gsem.h, g_add[j - 3])
                            vsem.tick(v.tensor_scalar(tb[:, 0:ncols], nrm[:, 0:ncols],
                                                      float(NF4_BOUNDS[j]), float(2.0 * NF4_GAMMA[j]),
                                                      ALU.is_gt, ALU.mult))
                        v_term[j] = vsem.n

                        if j < NGP:
                            @block.gpsimd
                            def _(g, j=j, tb=tb):
                                g.wait_ge(vsem.h, v_term[j])
                                if j == 0:
                                    gsem.tick(g.tensor_copy(acc[:, 0:ncols], tb[:, 0:ncols]))
                                else:
                                    gsem.tick(g.tensor_tensor(acc[:, 0:ncols], acc[:, 0:ncols],
                                                              tb[:, 0:ncols], ALU.add))
                            g_add[j] = gsem.n
                        else:
                            @block.vector
                            def _(v, j=j, tb=tb):
                                if j == NGP:
                                    cv = -float(np.sum(NF4_GAMMA, dtype=np.float64))
                                    vsem.tick(v.tensor_scalar(vacc[:, 0:ncols], tb[:, 0:ncols],
                                                              cv, None, ALU.add))
                                else:
                                    vsem.tick(v.tensor_tensor(vacc[:, 0:ncols], vacc[:, 0:ncols],
                                                              tb[:, 0:ncols], ALU.add))
                            v_term[j] = vsem.n
                    state["s_last_sign"] = ssem.n

                    @block.vector
                    def _(v):
                        if NGP > 0:
                            v.wait_ge(gsem.h, g_add[NGP - 1])
                            v.tensor_tensor(acc[:, 0:ncols], acc[:, 0:ncols], vacc[:, 0:ncols], ALU.add)
                            v.wait_ge(vsem.h, vsem.n)
                            src_acc = acc
                        else:
                            src_acc = vacc
                        vsem.tick(v.tensor_tensor(
                            r3(dqb[:, 0:ncols]), r3(src_acc[:, 0:ncols]),
                            am[:, 0:nblk].rearrange("p (n o) -> p n o", o=1).broadcast_to([128, nblk, BLK]),
                            ALU.mult))
                    v_dqb = vsem.n

                    ngroups = (ngrp + 7) // 8
                    for g in range(ngroups):
                        n_in_g = min(8, ngrp - g * 8)

                        @block.tensor
                        def _(t, g=g, n_in_g=n_in_g):
                            t.wait_ge(vsem.h, v_dqb)
                            if state["v_ev_prev"]:
                                t.wait_ge(ssem.h, state["v_ev_prev"])
                            last = None
                            for i in range(n_in_g):
                                c = (g * 8 + i) * 128
                                last = t.transpose(tps[:, i * 128:(i + 1) * 128],
                                                   dqb[:, c:c + 128], ident[:])
                            pesem.tick(last)
                        pe_g = pesem.n

                        @block.scalar
                        def _(sc, n_in_g=n_in_g, pe_g=pe_g):
                            sc.wait_ge(pesem.h, pe_g)
                            if state["dq_ev_st"]:
                                sc.wait_ge(stdq.h, state["dq_ev_st"])
                            ssem.tick(sc.copy(ev[:, 0:n_in_g * 128], tps[:, 0:n_in_g * 128]))
                        state["v_ev_prev"] = ssem.n
                        v_ev = ssem.n

                        @block.sync
                        def _(sync, g=g, n_in_g=n_in_g, v_ev=v_ev):
                            sync.wait_ge(ssem.h, v_ev)
                            for i in range(n_in_g):
                                stdq.tick(sync.dma_start(out=writes[g * 8 + i],
                                                         in_=ev[:, i * 128:(i + 1) * 128]))
                        state["dq_ev_st"] = stdq.n
                    state["pe_last"] = pesem.n

                tile_id = 0
                for wsrc, dqdst in ((w1_sh, dq1t_c), (w2_sh, dq2t_c)):
                    for k in range(SPC):
                        slot_rows = slice(k * 128, (k + 1) * 128)
                        writes = [dqdst[k * 128:(k + 1) * 128, dt * 128:(dt + 1) * 128] for dt in range(NDT)]
                        dq_tile(wsrc[k * 128:(k + 1) * 128, :], D, writes, tile_id)
                        tile_id += 1
                for dti in range(NDT):
                    writes = [dq3t_c[k * 128:(k + 1) * 128, dti * 128:(dti + 1) * 128] for k in range(SPC)]
                    dq_tile(w3_sh[dti * 128:(dti + 1) * 128, :], SPC * 128, writes, tile_id)
                    tile_id += 1

                dq_stores = stdq.n

                @block.gpsimd
                def _(g):
                    g.wait_ge(stdq.h, dq_stores)
                    cc_dq.tick(g.collective_compute(
                        "AllGather", ALU.bypass, replica_groups=[list(range(NCORES))],
                        ins=[dq1t_c[:].opt()], outs=[dq1t_all[:].opt()]))
                    cc_dq.tick(g.collective_compute(
                        "AllGather", ALU.bypass, replica_groups=[list(range(NCORES))],
                        ins=[dq2t_c[:].opt()], outs=[dq2t_all[:].opt()]))
                    cc_dq.tick(g.collective_compute(
                        "AllGather", ALU.bypass, replica_groups=[list(range(NCORES))],
                        ins=[dq3t_c[:].opt()], outs=[dq3t_all[:].opt()]))

        if nphases >= 2:
            for b in range(B):
                with ExitStack() as ph:
                    wqb = ph.enter_context(nc.sbuf_tensor(f"p1_wq_{b}", [128, NDT * 512], BF16))
                    wkb = ph.enter_context(nc.sbuf_tensor(f"p1_wk_{b}", [128, NDT * 512], BF16))
                    wvb = ph.enter_context(nc.sbuf_tensor(f"p1_wv_{b}", [128, NDT * 512], BF16))
                    xcb = ph.enter_context(nc.sbuf_tensor(f"p1_xcb_{b}", [128, NDT * 1024], BF16))
                    cosb = ph.enter_context(nc.sbuf_tensor(f"p1_cos_{b}", [128, 8 * 128], F32))
                    sinb = ph.enter_context(nc.sbuf_tensor(f"p1_sin_{b}", [128, 8 * 128], F32))
                    qf = ph.enter_context(nc.sbuf_tensor(f"p1_qf_{b}", [128, 512], F32))
                    kf = ph.enter_context(nc.sbuf_tensor(f"p1_kf_{b}", [128, 512], F32))
                    rtmpA = ph.enter_context(nc.sbuf_tensor(f"p1_tA_{b}", [128, 256], F32))
                    rtmpB = ph.enter_context(nc.sbuf_tensor(f"p1_tB_{b}", [128, 256], F32))
                    qbt = ph.enter_context(nc.sbuf_tensor(f"p1_qb_{b}", [128, 512], BF16))
                    kbt = ph.enter_context(nc.sbuf_tensor(f"p1_kb_{b}", [128, 512], BF16))
                    vbt = ph.enter_context(nc.sbuf_tensor(f"p1_vb_{b}", [128, 512], BF16))
                    ev = ph.enter_context(nc.sbuf_tensor(f"p1_ev_{b}", [128, 1024], BF16))
                    qps = ph.enter_context(nc.psum_tensor(f"p1_qps_{b}", [128, 512], F32))
                    kps = ph.enter_context(nc.psum_tensor(f"p1_kps_{b}", [128, 512], F32))
                    vps = ph.enter_context(nc.psum_tensor(f"p1_vps_{b}", [128, 512], F32))
                    tps = ph.enter_context(nc.psum_tensor(f"p1_tps_{b}", [128, 1024], BF16))

                    pe_now, v_now, s_now = pesem.n, vsem.n, ssem.n

                    @block.gpsimd
                    def _(g, pe_now=pe_now, v_now=v_now, s_now=s_now):
                        g.wait_ge(pesem.h, pe_now)
                        g.wait_ge(vsem.h, v_now)
                        g.wait_ge(ssem.h, s_now)
                        for wsb, wsrc in ((wqb, wq_sl), (wkb, wk_sl), (wvb, wv_sl)):
                            for dt in range(NDT):
                                gwc.tick(g.dma_start(out=wsb[:, dt * 512:(dt + 1) * 512],
                                                     in_=wsrc[dt * 128:(dt + 1) * 128, :]))
                    w_ld = gwc.n

                    st_evac = {}
                    st_tp = {}
                    st_store = {}
                    v_rope = {}
                    pe_mm = {}
                    pe_tp = {}

                    for cb in range(4):
                        blk = b * 4 + cb

                        @block.sync
                        def _(sync, cb=cb, blk=blk):
                            if (b, cb) != (0, 0):
                                sync.wait_ge(pesem.h, pesem.n)
                            for dt in range(NDT):
                                ldxc.tick(sync.dma_start(
                                    out=xcb[:, dt * 1024:(dt + 1) * 1024],
                                    in_=xnt_all[blk * D + dt * 128: blk * D + (dt + 1) * 128, :]))
                            cs = cos_t[cb * 1024:(cb + 1) * 1024, :].rearrange("(t p) h -> p t h", p=128)
                            sn = sin_t[cb * 1024:(cb + 1) * 1024, :].rearrange("(t p) h -> p t h", p=128)
                            ldxc.tick(sync.dma_start(out=cosb[:].rearrange("p (t h) -> p t h", h=128), in_=cs))
                            ldxc.tick(sync.dma_start(out=sinb[:].rearrange("p (t h) -> p t h", h=128), in_=sn))
                        x_ld = ldxc.n

                        for tt in range(NTT):
                            if P1LVL < 3:
                                continue
                            @block.tensor
                            def _(t, cb=cb, tt=tt, x_ld=x_ld):
                                t.wait_ge(ldxc.h, x_ld)
                                if (cb, tt) == (0, 0):
                                    t.wait_ge(gwc.h, w_ld)
                                prev = (cb, tt - 1) if tt > 0 else ((cb - 1, NTT - 1) if cb > 0 else None)
                                if prev is not None and st_evac.get(prev):
                                    t.wait_ge(ssem.h, st_evac[prev])
                                last = None
                                for dt in range(NDT):
                                    lt = xcb[:, dt * 1024 + tt * 128: dt * 1024 + (tt + 1) * 128]
                                    st_ = (dt == 0)
                                    sp = (dt == NDT - 1)
                                    t.matmul(qps[:], lt, wqb[:, dt * 512:(dt + 1) * 512], start=st_, stop=sp)
                                    t.matmul(kps[:], lt, wkb[:, dt * 512:(dt + 1) * 512], start=st_, stop=sp)
                                    last = t.matmul(vps[:], lt, wvb[:, dt * 512:(dt + 1) * 512], start=st_, stop=sp)
                                pesem.tick(last)
                            pe_mm[(cb, tt)] = pesem.n

                            if P1LVL >= 35:
                                @block.scalar
                                def _(s, cb=cb, tt=tt):
                                    s.wait_ge(pesem.h, pe_mm[(cb, tt)])
                                    prev = (cb, tt - 1) if tt > 0 else ((cb - 1, NTT - 1) if cb > 0 else None)
                                    if prev is not None and st_store.get(prev):
                                        s.wait_ge(stqk.h, st_store[prev])
                                    if (cb, tt) != (0, 0) and vsem.n:
                                        s.wait_ge(vsem.h, vsem.n)
                                    if P1LVL == 34:
                                        ssem.tick(s.copy(qf[:, 0:NCONST], consts[:]))
                                    else:
                                        ssem.tick(s.copy(qf[:], qps[:]))
                                    if P1LVL >= 36:
                                        ssem.tick(s.copy(kf[:], kps[:]))
                                        ssem.tick(s.copy(vbt[:], vps[:]))
                                st_evac[(cb, tt)] = ssem.n
                            else:
                                st_evac[(cb, tt)] = ssem.n

                            if P1LVL < 40:
                                continue
                            @block.vector
                            def _(v, cb=cb, tt=tt):
                                v.wait_ge(ssem.h, st_evac[(cb, tt)])
                                if (cb, tt) != (0, 0):
                                    v.wait_ge(pesem.h, pesem.n)
                                for src, dst in ((qf, qbt), (kf, kbt)):
                                    s4 = src[:].rearrange("p (h a x) -> p h a x", h=HPC, a=2)
                                    d4 = dst[:].rearrange("p (h a x) -> p h a x", h=HPC, a=2)
                                    c4 = cosb[:, tt * 128:(tt + 1) * 128].rearrange("p (o a x) -> p o a x", o=1, a=2)
                                    s4t = sinb[:, tt * 128:(tt + 1) * 128].rearrange("p (o a x) -> p o a x", o=1, a=2)
                                    c1 = c4[:, :, 0, :].broadcast_to([128, HPC, HH])
                                    c2 = c4[:, :, 1, :].broadcast_to([128, HPC, HH])
                                    s1 = s4t[:, :, 0, :].broadcast_to([128, HPC, HH])
                                    s2 = s4t[:, :, 1, :].broadcast_to([128, HPC, HH])
                                    tA = rtmpA[:].rearrange("p (h x) -> p h x", h=HPC)
                                    tB = rtmpB[:].rearrange("p (h x) -> p h x", h=HPC)
                                    q1 = s4[:, :, 0, :]
                                    q2 = s4[:, :, 1, :]
                                    v.tensor_tensor(tA, q1, c1, ALU.mult)
                                    v.tensor_tensor(tB, q2, s1, ALU.mult)
                                    v.tensor_tensor(d4[:, :, 0, :], tA, tB, ALU.subtract)
                                    v.tensor_tensor(tA, q2, c2, ALU.mult)
                                    v.tensor_tensor(tB, q1, s2, ALU.mult)
                                    last = v.tensor_tensor(d4[:, :, 1, :], tA, tB, ALU.add)
                                vsem.tick(last)
                            v_rope[(cb, tt)] = vsem.n

                            if P1LVL < 50:
                                continue
                            @block.tensor
                            def _(t, cb=cb, tt=tt):
                                t.wait_ge(vsem.h, v_rope[(cb, tt)])
                                if (cb, tt) != (0, 0):
                                    t.wait_ge(ssem.h, ssem.n)
                                for h in range(HPC):
                                    t.transpose(tps[:, h * 128:(h + 1) * 128],
                                                qbt[:, h * 128:(h + 1) * 128], ident[:])
                                last = None
                                for h in range(HPC):
                                    last = t.transpose(tps[:, 512 + h * 128: 512 + (h + 1) * 128],
                                                       kbt[:, h * 128:(h + 1) * 128], ident[:])
                                pesem.tick(last)
                            pe_tp[(cb, tt)] = pesem.n

                            @block.scalar
                            def _(s, cb=cb, tt=tt):
                                s.wait_ge(pesem.h, pe_tp[(cb, tt)])
                                if (cb, tt) != (0, 0):
                                    s.wait_ge(stqk.h, stqk.n)
                                ssem.tick(s.copy(ev[:], tps[:]))
                            st_tp[(cb, tt)] = ssem.n

                            @block.sync
                            def _(sync, cb=cb, tt=tt, blk=blk):
                                sync.wait_ge(ssem.h, st_tp[(cb, tt)])
                                col = cb * 1024 + tt * 128
                                for h in range(HPC):
                                    r0 = (b * HPC + h) * HD
                                    stqk.tick(sync.dma_start(
                                        out=qt_sp[r0:r0 + 128, col:col + 128],
                                        in_=ev[:, h * 128:(h + 1) * 128]))
                                    stqk.tick(sync.dma_start(
                                        out=kt_sp[r0:r0 + 128, col:col + 128],
                                        in_=ev[:, 512 + h * 128: 512 + (h + 1) * 128]))
                                    kt_glob = cb * 8 + tt
                                    vr = (b * HPC + h) * 128
                                    stqk.tick(sync.dma_start(
                                        out=v_sp[vr:vr + 128, kt_glob * 128:(kt_glob + 1) * 128],
                                        in_=vbt[:, h * 128:(h + 1) * 128]))
                            st_store[(cb, tt)] = stqk.n

                    p1_stores = stqk.n
                    p1_pe = pesem.n

                if nphases < 3:
                    continue
                with ExitStack() as ph:
                    qth = [ph.enter_context(nc.sbuf_tensor(f"p2_q{b}_{i}", [128, S], BF16)) for i in range(2)]
                    kth = [ph.enter_context(nc.sbuf_tensor(f"p2_k{b}_{i}", [128, S], BF16)) for i in range(2)]
                    vh = [ph.enter_context(nc.sbuf_tensor(f"p2_v{b}_{i}", [128, 32 * 128], BF16)) for i in range(2)]
                    perow = [ph.enter_context(nc.sbuf_tensor(f"p2_pe_{b}_{i}", [128, S], F32)) for i in range(2)]
                    pnrow = [ph.enter_context(nc.sbuf_tensor(f"p2_pn_{b}_{i}", [128, S], BF16)) for i in range(4)]
                    dcol = ph.enter_context(nc.sbuf_tensor(f"p2_dc_{b}", [128, 16], F32))
                    ptb = ph.enter_context(nc.sbuf_tensor(f"p2_pt_{b}", [128, 32 * 512], BF16))
                    dstat = ph.enter_context(nc.sbuf_tensor(f"p2_d_{b}", [128, 4], F32))
                    cxb = ph.enter_context(nc.sbuf_tensor(f"p2_cx_{b}", [128, 512], BF16))
                    sps = [ph.enter_context(nc.psum_tensor(f"p2_s{b}_{i}", [128, 512], F32)) for i in range(4)]
                    tps2 = ph.enter_context(nc.psum_tensor(f"p2_tps_{b}", [128, 2048], BF16))
                    ctxps = ph.enter_context(nc.psum_tensor(f"p2_ctx_{b}", [128, 512], F32))

                    hd_ld = {}
                    hd_pe_done = {}
                    prev_head_vnorm = 0
                    for h in range(HPC):
                        @block.sync
                        def _(sync, h=h):
                            if h == 0:
                                sync.wait_ge(stqk.h, p1_stores)
                                sync.wait_ge(pesem.h, pesem.n)
                                sync.wait_ge(ssem.h, ssem.n)
                                sync.wait_ge(vsem.h, vsem.n)
                            if h >= 2:
                                sync.wait_ge(pesem.h, hd_pe_done[h - 2])
                            r0 = (b * HPC + h) * HD
                            ldh[h % 2].tick(sync.dma_start(out=qth[h % 2][:], in_=qt_sp[r0:r0 + 128, :]))
                            ldh[h % 2].tick(sync.dma_start(out=kth[h % 2][:], in_=kt_sp[r0:r0 + 128, :]))
                            vr = (b * HPC + h) * 128
                            ldh[h % 2].tick(sync.dma_start(out=vh[h % 2][:], in_=v_sp[vr:vr + 128, :]))
                        hd_ld[h] = ldh[h % 2].n

                        qq = qth[h % 2]
                        kk = kth[h % 2]
                        vv = vh[h % 2]

                        s_exp_list = []
                        v_norm = {}
                        s_ptev = {}
                        ck_counter = [0]

                        def emit_scores(qt):
                            kext = 128 * (qt + 1)
                            nkc = (kext + 511) // 512
                            for kc in range(nkc):
                                wdt = min(512, kext - kc * 512)
                                ck = ck_counter[0]
                                ck_counter[0] += 1

                                @block.tensor
                                def _(t, qt=qt, kc=kc, wdt=wdt, ck=ck, nkc=nkc):
                                    if (qt, kc) == (0, 0):
                                        t.wait_ge(ldh[h % 2].h, hd_ld[h])
                                    if ck >= 4:
                                        t.wait_ge(ssem.h, s_exp_list[ck - 4])
                                    last = t.matmul(sps[ck % 4][:, 0:wdt],
                                             qq[:, qt * 128:(qt + 1) * 128],
                                             kk[:, kc * 512: kc * 512 + wdt],
                                             start=True, stop=(kc != nkc - 1))
                                    if kc == nkc - 1:
                                        last = t.matmul(sps[ck % 4][:, wdt - 128:wdt],
                                                        ident[:], maskb[:], start=False, stop=True)
                                    pesem.tick(last)
                                pe_s = pesem.n

                                @block.scalar
                                def _(s, qt=qt, kc=kc, wdt=wdt, ck=ck, pe_s=pe_s,
                                      pv=prev_head_vnorm):
                                    s.wait_ge(pesem.h, pe_s)
                                    if kc == 0:
                                        s.wait_ge(vsem.h, v_norm.get(qt - 2, pv))
                                    ssem.tick(s.activation(
                                        perow[qt % 2][:, kc * 512: kc * 512 + wdt],
                                        sps[ck % 4][:, 0:wdt], AF.Exp,
                                        bias=consts[:, C_NEGM:C_NEGM + 1],
                                        scale=consts[:, C_SCALE:C_SCALE + 1],
                                        accum_out=dcol[:, (qt % 2) * 8 + kc:(qt % 2) * 8 + kc + 1]))
                                s_exp_list.append(ssem.n)

                            @block.vector
                            def _(v, qt=qt, kext=kext, nkc=nkc):
                                v.wait_ge(ssem.h, s_exp_list[-1])
                                vsem.tick(v.tensor_reduce(dstat[:, 0:1],
                                                          dcol[:, (qt % 2) * 8:(qt % 2) * 8 + nkc],
                                                          AX.X, ALU.add))
                                v.wait_ge(vsem.h, vsem.n)
                                vsem.tick(v.reciprocal(dstat[:, 1:2], dstat[:, 0:1]))
                                v.wait_ge(vsem.h, vsem.n)
                                vsem.tick(v.tensor_scalar(pnrow[qt % 4][:, 0:kext], perow[qt % 2][:, 0:kext],
                                                          dstat[:, 1:2], None, ALU.mult))
                            v_norm[qt] = vsem.n

                        def emit_tp_pv(qt):
                            ntp = qt + 1
                            ngr = (ntp + 15) // 16
                            for gi in range(ngr):
                                n_in_g = min(16, ntp - gi * 16)

                                @block.tensor
                                def _(t, qt=qt, gi=gi, n_in_g=n_in_g):
                                    t.wait_ge(vsem.h, v_norm[qt])
                                    t.wait_ge(ssem.h, s_ptev_prev[0])
                                    last = None
                                    for i in range(n_in_g):
                                        kt = gi * 16 + i
                                        last = t.transpose(tps2[:, i * 128:(i + 1) * 128],
                                                           pnrow[qt % 4][:, kt * 128:(kt + 1) * 128], ident[:])
                                    pesem.tick(last)
                                pe_t = pesem.n

                                @block.scalar
                                def _(s, qt=qt, gi=gi, n_in_g=n_in_g, pe_t=pe_t):
                                    s.wait_ge(pesem.h, pe_t)
                                    dst = ptb[:].rearrange("p (k q) -> p k q", q=512)[
                                        :, gi * 16: gi * 16 + n_in_g, (qt % 4) * 128:(qt % 4) * 128 + 128]
                                    ssem.tick(s.copy(dst, tps2[:, 0:n_in_g * 128]
                                                     .rearrange("p (k q) -> p k q", q=128)))
                                s_ptev_prev[0] = ssem.n
                            s_ptev[qt] = ssem.n

                            if qt % 4 == 3:
                                qc = qt // 4

                                @block.tensor
                                def _(t, qc=qc, qt=qt):
                                    t.wait_ge(ssem.h, s_ptev[qt])
                                    nkt = 4 * qc + 4
                                    last = None
                                    for kt in range(nkt):
                                        off = max(0, (kt - 4 * qc) * 128)
                                        last = t.matmul(ctxps[:, off:512],
                                                 vv[:, kt * 128:(kt + 1) * 128],
                                                 ptb[:].rearrange("p (k q) -> p k q", q=512)[:, kt, off:512],
                                                 start=(kt == 0), stop=(kt == nkt - 1))
                                    pesem.tick(last)
                                pe_pv = pesem.n

                                @block.scalar
                                def _(s, qc=qc, pe_pv=pe_pv):
                                    s.wait_ge(pesem.h, pe_pv)
                                    if qc > 0:
                                        s.wait_ge(stcx.h, stcx.n)
                                    ssem.tick(s.copy(cxb[:], ctxps[:]))
                                s_cx = ssem.n

                                @block.sync
                                def _(sync, h=h, qc=qc, s_cx=s_cx, b=b):
                                    sync.wait_ge(ssem.h, s_cx)
                                    r0 = b * 512 + h * 128
                                    c0 = qc * 512
                                    stcx.tick(sync.dma_start(out=ctx_c[r0:r0 + 128, c0:c0 + 512],
                                                             in_=cxb[:]))

                        s_ptev_prev = [ssem.n]
                        for qc in range(8):
                            for qt in range(4 * qc, 4 * qc + 4):
                                emit_scores(qt)
                            for qt in range(4 * qc, 4 * qc + 4):
                                emit_tp_pv(qt)

                        hd_pe_done[h] = pesem.n
                        prev_head_vnorm = v_norm[31]

            if nphases >= 3:
                ctx_stores = stcx.n

                @block.gpsimd
                def _(g):
                    g.wait_ge(stcx.h, ctx_stores)
                    cc_ctx.tick(g.collective_compute(
                        "AllGather", ALU.bypass, replica_groups=[list(range(NCORES))],
                        ins=[ctx_c[:].opt()], outs=[ctx_all[:].opt()]))

        if nphases >= 4:
            with ExitStack() as ph:
                xnb4 = ph.enter_context(nc.sbuf_tensor("p4_xn", [128, NDT * 1024], BF16))
                hact = ph.enter_context(nc.sbuf_tensor("p4_ha", [128, 44 * 1024], BF16))
                dq1b = [ph.enter_context(nc.sbuf_tensor(f"p4_d1{i}", [128, D], BF16)) for i in range(2)]
                dq2b = [ph.enter_context(nc.sbuf_tensor(f"p4_d2{i}", [128, D], BF16)) for i in range(2)]
                dq3b = [ph.enter_context(nc.sbuf_tensor(f"p4_d3{i}", [128, 512], BF16)) for i in range(2)]
                sg = ph.enter_context(nc.sbuf_tensor("p4_sg", [128, 1024], F32))
                uf = ph.enter_context(nc.sbuf_tensor("p4_uf", [128, 1024], F32))
                yb = [ph.enter_context(nc.sbuf_tensor(f"p4_y{i}", [128, 512], F32)) for i in range(4)]
                pbank = [ph.enter_context(nc.psum_tensor(f"p4_pb{i}", [128, 512], F32)) for i in range(8)]

                pe_now, v_now, s_now = pesem.n, vsem.n, ssem.n

                @block.sync
                def _(sync):
                    sync.wait_ge(pesem.h, pe_now)
                    sync.wait_ge(vsem.h, v_now)
                    sync.wait_ge(ssem.h, s_now)
                    sync.wait_ge(cc_dq.h, cc_dq.n)
                    for dt in range(NDT):
                        ldxn4.tick(sync.dma_start(out=xnb4[:, dt * 1024:(dt + 1) * 1024],
                                                  in_=xnt_c[dt * 128:(dt + 1) * 128, :]))
                xn_ld = ldxn4.n

                chunks = [[s_ for s_ in REAL_SLOTS if s_ < 44], [s_ for s_ in REAL_SLOTS if s_ >= 44]]
                y_written = {}
                dq_ld = {}
                last_hact_v = 0
                for ci, chunk in enumerate(chunks):
                    s_silu = {}
                    for si, slot in enumerate(chunk):
                        @block.sync
                        def _(sync, ci=ci, si=si, slot=slot):
                            if ci == 0 and si == 0:
                                sync.wait_ge(ldxn4.h, xn_ld)
                            if si >= 2:
                                sync.wait_ge(pesem.h, dq_used[(ci, si - 2)])
                            elif ci > 0:
                                sync.wait_ge(pesem.h, dq_used[(ci - 1, len(chunks[ci - 1]) - 1)])
                            ldd12[si % 2].tick(sync.dma_start(out=dq1b[si % 2][:],
                                                      in_=dq1t_all[slot * 128:(slot + 1) * 128, :]))
                            ldd12[si % 2].tick(sync.dma_start(out=dq2b[si % 2][:],
                                                      in_=dq2t_all[slot * 128:(slot + 1) * 128, :]))
                        dq_ld[(ci, si)] = ldd12[si % 2].n
                        if ci == 0 and si == 0:
                            dq_used = {}

                        off = (si % 2) * 4

                        @block.tensor
                        def _(t, ci=ci, si=si, off=off):
                            t.wait_ge(ldd12[si % 2].h, dq_ld[(ci, si)])
                            if si >= 2:
                                t.wait_ge(ssem.h, s_silu[si - 2])
                            elif ci > 0:
                                t.wait_ge(ssem.h, ssem.n)
                            last = None
                            for dt in range(NDT):
                                st_ = (dt == 0); sp = (dt == NDT - 1)
                                l1 = dq1b[si % 2][:, dt * 128:(dt + 1) * 128]
                                l2 = dq2b[si % 2][:, dt * 128:(dt + 1) * 128]
                                t.matmul(pbank[off + 0][:], l1, xnb4[:, dt * 1024: dt * 1024 + 512], start=st_, stop=sp)
                                t.matmul(pbank[off + 1][:], l1, xnb4[:, dt * 1024 + 512: dt * 1024 + 1024], start=st_, stop=sp)
                                t.matmul(pbank[off + 2][:], l2, xnb4[:, dt * 1024: dt * 1024 + 512], start=st_, stop=sp)
                                last = t.matmul(pbank[off + 3][:], l2, xnb4[:, dt * 1024 + 512: dt * 1024 + 1024], start=st_, stop=sp)
                            pesem.tick(last)
                        pe_gu = pesem.n
                        dq_used[(ci, si)] = pe_gu

                        @block.scalar
                        def _(s, ci=ci, si=si, pe_gu=pe_gu, off=off):
                            s.wait_ge(pesem.h, pe_gu)
                            if last_hact_v:
                                s.wait_ge(vsem.h, last_hact_v)
                            s.activation(sg[:, 0:512], pbank[off + 0][:], AF.Silu)
                            s.activation(sg[:, 512:1024], pbank[off + 1][:], AF.Silu)
                            s.copy(uf[:, 0:512], pbank[off + 2][:])
                            ssem.tick(s.copy(uf[:, 512:1024], pbank[off + 3][:]))
                        s_silu[si] = ssem.n

                        @block.vector
                        def _(v, ci=ci, si=si):
                            v.wait_ge(ssem.h, s_silu[si])
                            vsem.tick(v.tensor_tensor(hact[:, si * 1024:(si + 1) * 1024],
                                                      sg[:], uf[:], ALU.mult))
                        last_hact_v = vsem.n

                    nsl = len(chunk)
                    for c in range(8):
                        d3_ld = {}
                        for si, slot in enumerate(chunk):
                            @block.sync
                            def _(sync, ci=ci, c=c, si=si, slot=slot):
                                key = None
                                if si >= 2:
                                    key = (ci, c, si - 2)
                                elif c > 0:
                                    key = (ci, c - 1, nsl - 1)
                                elif ci > 0:
                                    key = (ci - 1, 7, len(chunks[ci - 1]) - 1)
                                if key is not None and key in d3_used:
                                    sync.wait_ge(pesem.h, d3_used[key])
                                ldd3[si % 2].tick(sync.dma_start(
                                    out=dq3b[si % 2][:],
                                    in_=dq3t_all[slot * 128:(slot + 1) * 128, c * 512:(c + 1) * 512]))
                            d3_ld[si] = ldd3[si % 2].n
                            if (ci, c, si) == (0, 0, 0):
                                d3_used = {}

                            @block.tensor
                            def _(t, ci=ci, c=c, si=si, nsl=nsl):
                                t.wait_ge(ldd3[si % 2].h, d3_ld[si])
                                if si == 0:
                                    t.wait_ge(vsem.h, last_hact_v)
                                    if (ci, c) != (0, 0):
                                        t.wait_ge(ssem.h, ssem.n)
                                last = None
                                for tt in range(NTT):
                                    last = t.matmul(pbank[tt][:],
                                                    hact[:, si * 1024 + tt * 128: si * 1024 + (tt + 1) * 128],
                                                    dq3b[si % 2][:],
                                                    start=(si == 0), stop=(si == nsl - 1))
                                pesem.tick(last)
                            d3_used[(ci, c, si)] = pesem.n
                        pe_dn = pesem.n

                        for tt in range(NTT):
                            @block.scalar
                            def _(s, ci=ci, c=c, tt=tt, pe_dn=pe_dn):
                                s.wait_ge(pesem.h, pe_dn)
                                if sty.n:
                                    s.wait_ge(sty.h, sty.n)
                                if gy.n:
                                    s.wait_ge(gy.h, gy.n)
                                ssem.tick(s.copy(yb[tt % 2][:], pbank[tt][:]))
                            s_y = ssem.n

                            if ci == 0:
                                @block.sync
                                def _(sync, c=c, tt=tt, s_y=s_y):
                                    sync.wait_ge(ssem.h, s_y)
                                    sty.tick(sync.dma_start(
                                        out=y_hbm[tt * 128:(tt + 1) * 128, c * 512:(c + 1) * 512],
                                        in_=yb[tt % 2][:]))
                                y_written[(ci, c, tt)] = sty.n
                            else:
                                @block.gpsimd
                                def _(g, c=c, tt=tt, s_y=s_y, sty_all=None):
                                    g.wait_ge(ssem.h, s_y)
                                    g.wait_ge(sty.h, sty.n)
                                    gy.tick(g.dma_start(
                                        out=y_hbm[tt * 128:(tt + 1) * 128, c * 512:(c + 1) * 512],
                                        in_=yb[tt % 2][:], accum_op=ALU.add))
                                y_written[(ci, c, tt)] = gy.n

        if nphases >= 5:
            with ExitStack() as ph:
                ctxme = ph.enter_context(nc.sbuf_tensor("p3_cx", [128, NDT * 1024], BF16))
                cand = [ph.enter_context(nc.sbuf_tensor(f"p3_cd{i}", [128, 1024], BF16)) for i in range(2)]
                ctmp = ph.enter_context(nc.sbuf_tensor("p3_ct", [128, 1024], BF16))
                cmsk = ph.enter_context(nc.sbuf_tensor("p3_cm", [128, 8], F32))
                wob = [ph.enter_context(nc.sbuf_tensor(f"p3_wo{i}", [128, 512], BF16)) for i in range(2)]
                gbuf = [ph.enter_context(nc.sbuf_tensor(f"p3_g{i}", [128, 512], F32)) for i in range(4)]
                ybuf = [ph.enter_context(nc.sbuf_tensor(f"p3_y{i}", [128, 512], F32)) for i in range(4)]
                xbuf = [ph.enter_context(nc.sbuf_tensor(f"p3_x{i}", [128, 512], F32)) for i in range(4)]
                sf = ph.enter_context(nc.sbuf_tensor("p3_sf", [128, 512], F32))
                ob = [ph.enter_context(nc.sbuf_tensor(f"p3_o{i}", [128, 512], F32)) for i in range(4)]
                spank = [ph.enter_context(nc.psum_tensor(f"p3_pb{i}", [128, 512], F32)) for i in range(8)]

                pe_now, v_now, s_now = pesem.n, vsem.n, ssem.n

                @block.sync
                def _(sync):
                    sync.wait_ge(pesem.h, pe_now)
                    sync.wait_ge(vsem.h, v_now)
                    sync.wait_ge(ssem.h, s_now)
                    sync.wait_ge(cc_ctx.h, cc_ctx.n)
                    ldcd[0].tick(sync.dma_start(out=cmsk[:], in_=cmask_in[:]))

                v_asm = {}
                for dt in range(NDT):
                    cd_ld = {}
                    for j in range(NCORES):
                        @block.sync
                        def _(sync, dt=dt, j=j):
                            k = dt * 8 + j
                            if k >= 2:
                                pk = ((k - 2) // 8, (k - 2) % 8)
                                sync.wait_ge(vsem.h, v_asm[pk])
                            row0 = (dt // 4) * 1024 + (j // 4) * 512 + (dt % 4) * 128
                            col0 = (j % 4) * 1024
                            ldcd[j % 2].tick(sync.dma_start(out=cand[j % 2][:],
                                                      in_=ctx_all[row0:row0 + 128, col0:col0 + 1024]))
                        cd_ld[j] = ldcd[j % 2].n

                        @block.vector
                        def _(v, dt=dt, j=j):
                            v.wait_ge(ldcd[j % 2].h, cd_ld[j])
                            if j == 0:
                                vsem.tick(v.tensor_scalar(ctxme[:, dt * 1024:(dt + 1) * 1024],
                                                          cand[0][:], cmsk[:, 0:1], None, ALU.mult))
                            else:
                                v.tensor_scalar(ctmp[:], cand[j % 2][:], cmsk[:, j:j + 1], None, ALU.mult)
                                vsem.tick(v.tensor_tensor(ctxme[:, dt * 1024:(dt + 1) * 1024],
                                                          ctxme[:, dt * 1024:(dt + 1) * 1024],
                                                          ctmp[:], ALU.add))
                        v_asm[(dt, j)] = vsem.n
                asm_done = vsem.n

                wo_ld = {}
                out_written = {}
                for c in range(8):
                    for dt in range(NDT):
                        @block.gpsimd
                        def _(g, c=c, dt=dt):
                            k = c * NDT + dt
                            if k == 0:
                                g.wait_ge(vsem.h, asm_done)
                            if k >= 2:
                                pk = ((k - 2) // NDT, (k - 2) % NDT)
                                g.wait_ge(pesem.h, wo_used[pk])
                            gwo[dt % 2].tick(g.dma_start(out=wob[dt % 2][:],
                                                 in_=wo_full[dt * 128:(dt + 1) * 128, c * 512:(c + 1) * 512]))
                        wo_ld[(c, dt)] = gwo[dt % 2].n
                        if (c, dt) == (0, 0):
                            wo_used = {}

                        @block.tensor
                        def _(t, c=c, dt=dt):
                            t.wait_ge(gwo[dt % 2].h, wo_ld[(c, dt)])
                            if dt == 0 and c > 0:
                                t.wait_ge(ssem.h, ssem.n)
                            last = None
                            for tt in range(NTT):
                                last = t.matmul(spank[tt][:],
                                                ctxme[:, dt * 1024 + tt * 128: dt * 1024 + (tt + 1) * 128],
                                                wob[dt % 2][:],
                                                start=(dt == 0), stop=(dt == NDT - 1))
                            pesem.tick(last)
                        wo_used[(c, dt)] = pesem.n
                    pe_sp = pesem.n

                    for tt in range(NTT):
                        @block.sync
                        def _(sync, c=c, tt=tt):
                            k = c * NTT + tt
                            if k >= 2:
                                pk = ((k - 2) // NTT, (k - 2) % NTT)
                                sync.wait_ge(vsem.h, out_comb[pk])
                            ldyx[tt % 2].tick(sync.dma_start(out=ybuf[tt % 2][:],
                                                     in_=y_hbm[tt * 128:(tt + 1) * 128, c * 512:(c + 1) * 512]))
                            ldyx[tt % 2].tick(sync.dma_start(out=xbuf[tt % 2][:],
                                                     in_=x_own[tt * 128:(tt + 1) * 128, c * 512:(c + 1) * 512]))
                        yx_ld = ldyx[tt % 2].n
                        if (c, tt) == (0, 0):
                            out_comb = {}

                        @block.scalar
                        def _(s, c=c, tt=tt, yx_ld=yx_ld, pe_sp=pe_sp):
                            s.wait_ge(pesem.h, pe_sp)
                            s.wait_ge(ldyx[tt % 2].h, yx_ld)
                            if y_written:
                                s.wait_ge(gy.h, max(v for k, v in y_written.items() if k[0] == 1))
                            if (c, tt) != (0, 0):
                                s.wait_ge(vsem.h, vsem.n)
                            ssem.tick(s.activation(gbuf[tt % 2][:], ybuf[tt % 2][:], AF.Silu))
                            ssem.tick(s.copy(sf[:], spank[tt][:]))
                        s_fin = ssem.n

                        @block.vector
                        def _(v, c=c, tt=tt, s_fin=s_fin):
                            v.wait_ge(ssem.h, s_fin)
                            v.tensor_tensor(ob[tt % 2][:], sf[:], gbuf[tt % 2][:], ALU.mult)
                            vsem.tick(v.tensor_tensor(ob[tt % 2][:], ob[tt % 2][:], xbuf[tt % 2][:], ALU.add))
                        out_comb[(c, tt)] = vsem.n

                        @block.sync
                        def _(sync, c=c, tt=tt):
                            sync.wait_ge(vsem.h, out_comb[(c, tt)])
                            outsem.tick(sync.dma_start(
                                out=out[tt * 128:(tt + 1) * 128, c * 512:(c + 1) * 512],
                                in_=ob[tt % 2][:]))

        @block.sync
        def _(sync):
            for ctr in (cc_xnt, cc_dq, cc_ctx):
                if ctr.n:
                    sync.wait_ge(ctr.h, ctr.n)
            if nphases < 5:
                for rr in range(TC // 128):
                    stw.tick(sync.dma_start(out=out[rr * 128:(rr + 1) * 128, 0:NCONST], in_=consts[:]))
            for name, (tp, src, shape) in tap_out.items():
                rows = shape[0]
                step = max(128, rows // 8)
                r = 0
                while r < rows:
                    n = min(step, rows - r)
                    stw.tick(sync.dma_start(out=tp[r:r + n, :], in_=src[r:r + n, :]))
                    r += n
            for ctr in (stw, gld, ld, stxnt, stdq, stqk, stcx, sty, gy, outsem,
                        ldx[0], ldx[1], ldh[0], ldh[1], ldd12[0], ldd12[1],
                        ldd3[0], ldd3[1], ldcd[0], ldcd[1], gwo[0], gwo[1],
                        ldyx[0], ldyx[1], gwc, ldxc, ldw, ldxn4, ldyx[0]):
                if ctr.n:
                    sync.wait_ge(ctr.h, ctr.n)

    return nc, [n for n in tap_out]


def _shard_inputs(L0_attn_hidden_states, L0_attn_wq, L0_attn_wk, L0_attn_wv,
                  L0_attn_wo, L0_ffn_w1, L0_ffn_w2, L0_ffn_w3):
    x = np.ascontiguousarray(np.asarray(L0_attn_hidden_states, dtype=np.float32).reshape(B * S, D))
    wq = np.asarray(L0_attn_wq, dtype=np.float32)
    wk = np.asarray(L0_attn_wk, dtype=np.float32)
    wv = np.asarray(L0_attn_wv, dtype=np.float32)
    wo = np.ascontiguousarray(np.asarray(L0_attn_wo, dtype=np.float32))
    w1 = np.asarray(L0_ffn_w1, dtype=np.float32)
    w2 = np.asarray(L0_ffn_w2, dtype=np.float32)
    w3 = np.asarray(L0_ffn_w3, dtype=np.float32)

    cos_np, sin_np = _rope_tables()
    cb = _const_bank()
    ident_np = np.eye(128, dtype=np.float32)
    mask_np = _mask_tile()

    in_maps = []
    for c in range(NCORES):
        t0, t1 = CORE_TILE_START[c] * 128, (CORE_TILE_START[c] + CORE_TILES[c]) * 128
        npad = SPC * 128 - (t1 - t0)
        w1s = np.ascontiguousarray(w1[t0:t1, :])
        w2s = np.ascontiguousarray(w2[t0:t1, :])
        w3s = np.ascontiguousarray(w3[:, t0:t1])
        if npad:
            w1s = np.concatenate([w1s, w1s[:npad]], axis=0)
            w2s = np.concatenate([w2s, w2s[:npad]], axis=0)
            w3s = np.concatenate([w3s, w3s[:, :npad]], axis=1)
        in_maps.append({
            "x_own": np.ascontiguousarray(x[c * TC:(c + 1) * TC, :]),
            "wq_sl": np.ascontiguousarray(wq[:, c * 512:(c + 1) * 512]),
            "wk_sl": np.ascontiguousarray(wk[:, c * 512:(c + 1) * 512]),
            "wv_sl": np.ascontiguousarray(wv[:, c * 512:(c + 1) * 512]),
            "wo_full": wo,
            "w1_sh": np.ascontiguousarray(w1s),
            "w2_sh": np.ascontiguousarray(w2s),
            "w3_sh": np.ascontiguousarray(w3s),
            "cos_t": cos_np, "sin_t": sin_np, "cbank": cb,
            "ident_in": ident_np, "mask_in": mask_np,
            "cmask_in": np.eye(8, dtype=np.float32)[c][None, :].repeat(128, 0),
        })
    return in_maps


def kernel(**inputs):
    in_maps = _shard_inputs(**inputs)
    nc, _ = build_program()
    res = run_bass_kernel_spmd(nc, in_maps, core_ids=list(range(NCORES)))
    outs = [np.asarray(res.results[c]["out"]) for c in range(NCORES)]
    return np.concatenate(outs, axis=0).reshape(B, S, D)
